# version 17
# baseline (speedup 1.0000x reference)
import math
import numpy as np
import ml_dtypes
import concourse.bass as bass
import concourse.mybir as mybir
from concourse.bass_utils import run_bass_kernel_spmd
from concourse.alu_op_type import AluOpType
from contextlib import ExitStack

F32 = mybir.dt.float32
BF16 = mybir.dt.bfloat16
I8 = mybir.dt.int8
AF = mybir.ActivationFunctionType
AX = mybir.AxisListType

B, S, D = 4, 2048, 2048
NHL = 16
NHP = NHL // 2
NQT = S // 128
NDC = D // 128
DFF = 8192
NHC = DFF // 128
RLO = 16
TOK = 1024
BF = ml_dtypes.bfloat16


def _bucket(d):
    rp = max(d, 0)
    if rp < 16:
        return rp
    v = 16 + int(np.float32(np.log(np.float32(rp) / 16.0)) / np.float32(math.log(8.0)) * 16)
    return min(v, 31)


_HOST_CONST_CACHE = {}


def _vones():
    v = np.zeros((128, 1040), dtype=BF)
    v[:, 64::65] = 1
    return v


def _host_consts():
    if "bt" in _HOST_CONST_CACHE:
        return _HOST_CONST_CACHE
    bt = np.zeros((32, 2 * 32768), dtype=BF)
    for sec, dlt in enumerate((0, 128)):
        base = sec * 32768
        for i in range(128):
            row = base + i * 256
            for j in range(128):
                bt[_bucket(dlt + j - i), row + j] = 1
    _HOST_CONST_CACHE.update(
        bt=bt,
        ident=np.eye(128, dtype=BF),
        ones_f=np.ones((1, 128), dtype=np.float32),
        ones_bf=np.ones((1, 128), dtype=BF),
        sel2=np.concatenate([
            np.concatenate([np.ones((1, 64), BF), np.zeros((1, 64), BF)], axis=1),
            np.concatenate([np.zeros((1, 64), BF), np.ones((1, 64), BF)], axis=1)], axis=0),
        vones=_vones(),
        epsc=np.full((128, 1), 1e-5, dtype=np.float32),
    )
    return _HOST_CONST_CACHE


def _per_core_inputs(inp):
    x = np.ascontiguousarray(np.asarray(inp["L0_attn_hidden_states"], dtype=np.float32))
    Wq = np.asarray(inp["L0_attn_Wq"], dtype=np.float32)
    Wk = np.asarray(inp["L0_attn_Wk"], dtype=np.float32)
    Wv = np.asarray(inp["L0_attn_Wv"], dtype=np.float32)
    Wo = np.asarray(inp["L0_attn_Wo"], dtype=np.float32)
    relb = np.asarray(inp["L0_attn_rel_bias"], dtype=np.float32)
    wupq = np.ascontiguousarray(np.asarray(inp["L0_ffn_w_up_q"]).view(np.int8).reshape(DFF, D, 4)[:, :, 0])
    sup = np.asarray(inp["L0_ffn_w_up_scale"], dtype=np.float32)
    bup = np.asarray(inp["L0_ffn_b_up"], dtype=np.float32)
    aup = np.asarray(inp["L0_ffn_w_up_lora_a"], dtype=np.float32)
    bupl = np.asarray(inp["L0_ffn_w_up_lora_b"], dtype=np.float32)
    wdnq = np.ascontiguousarray(np.asarray(inp["L0_ffn_w_down_q"]).view(np.int8).reshape(D, DFF, 4)[:, :, 0])
    sdn = np.asarray(inp["L0_ffn_w_down_scale"], dtype=np.float32)
    bdn = np.asarray(inp["L0_ffn_b_down"], dtype=np.float32)
    adn = np.asarray(inp["L0_ffn_w_down_lora_a"], dtype=np.float32)
    bdnl = np.asarray(inp["L0_ffn_w_down_lora_b"], dtype=np.float32)
    hc = _host_consts()

    wo_r = np.ascontiguousarray(Wo.reshape(NDC, 128, 8, 256).transpose(2, 1, 0, 3))
    wup_r = np.ascontiguousarray(wupq.T.reshape(NDC, 128, 32, 256).transpose(2, 1, 0, 3))
    sup_r = np.ascontiguousarray(sup.T.reshape(NDC, 2, 32, 256).transpose(2, 1, 0, 3))
    wdn_r = np.ascontiguousarray(wdnq.T.reshape(NHC, 128, D))
    sdn_r = np.ascontiguousarray(sdn.T.reshape(NHC, 2, D))
    bup_c = np.ascontiguousarray(bup.reshape(NHC, 128).T)
    aup_r = np.ascontiguousarray(aup.reshape(NDC, 128, RLO).transpose(1, 0, 2).reshape(128, NDC * RLO))
    bupl_r = np.ascontiguousarray(bupl.reshape(RLO, 32, 256).transpose(1, 0, 2))
    adn_r = np.ascontiguousarray(adn.reshape(NHC, 128, RLO).transpose(1, 0, 2).reshape(128, NHC * RLO))
    bdn_row = np.ascontiguousarray(bdn.reshape(1, D))
    bdnl_r = np.ascontiguousarray(bdnl)

    maps = []
    for c in range(8):
        b, H = c // 2, c % 2
        cols = slice(H * 1024, (H + 1) * 1024)
        heads = slice(H * NHL, (H + 1) * NHL)
        crow = np.empty((1, 2 * NHL), dtype=np.float32)
        crow[0, 0::2] = relb[0, heads]
        crow[0, 1::2] = relb[31, heads]
        maps.append({
            "x_full": x[b],
            "x_own": np.ascontiguousarray(x[b, H * 1024:(H + 1) * 1024]),
            "wq_r": np.ascontiguousarray(Wq[:, cols].reshape(NDC, 128, 8, 128).transpose(2, 1, 0, 3)),
            "wk_r": np.ascontiguousarray(Wk[:, cols].reshape(NDC, 128, 8, 128).transpose(2, 1, 0, 3)),
            "wv_r": np.ascontiguousarray(Wv[:, cols].reshape(NDC, 128, 4, 256).transpose(2, 1, 0, 3)),
            "wo_r": wo_r,
            "rel_loc": np.ascontiguousarray(relb[:, heads]),
            "crow": crow,
            "wup_r": wup_r, "sup_r": sup_r, "bup_c": bup_c,
            "aup_r": aup_r, "bupl_r": bupl_r,
            "wdn_r": wdn_r, "sdn_r": sdn_r, "bdn_row": bdn_row,
            "adn_r": adn_r, "bdnl_r": bdnl_r,
            "bt": hc["bt"], "ident": hc["ident"], "ones_f": hc["ones_f"],
            "ones_bf": hc["ones_bf"], "sel2": hc["sel2"], "vones": hc["vones"],
            "epsc": hc["epsc"],
        })
    return maps


class KB:
    def __init__(self, nc, stack):
        self.nc = nc
        self.stack = stack
        self.sems = {}
        self.val = {}

    def sem(self, name):
        if name not in self.sems:
            self.sems[name] = self.stack.enter_context(self.nc.semaphore(name))
            self.val[name] = 0
        return self.sems[name]

    def d(self, instr, name):
        s = self.sem(name)
        self.val[name] += 16
        instr.then_inc(s, 16)

    def c(self, instr, name):
        s = self.sem(name)
        self.val[name] += 1
        instr.then_inc(s, 1)

    def flush(self, eng, instr, name="slf"):
        self.c(instr, name)
        eng.wait_ge(self.sem(name), self.val[name])


def build_program():
    nc = bass.Bass("TRN2", target_bir_lowering=False)
    TD = nc.dram_tensor
    x_full = TD("x_full", [S, D], F32, kind="ExternalInput")
    x_own = TD("x_own", [TOK, D], F32, kind="ExternalInput")
    wq_r = TD("wq_r", [8, 128, NDC, 128], F32, kind="ExternalInput")
    wk_r = TD("wk_r", [8, 128, NDC, 128], F32, kind="ExternalInput")
    wv_r = TD("wv_r", [4, 128, NDC, 256], F32, kind="ExternalInput")
    wo_r = TD("wo_r", [8, 128, NDC, 256], F32, kind="ExternalInput")
    rel_loc = TD("rel_loc", [32, NHL], F32, kind="ExternalInput")
    crow = TD("crow", [1, 2 * NHL], F32, kind="ExternalInput")
    wup_r = TD("wup_r", [32, 128, NDC, 256], I8, kind="ExternalInput")
    sup_r = TD("sup_r", [32, 2, NDC, 256], F32, kind="ExternalInput")
    bup_c = TD("bup_c", [128, NHC], F32, kind="ExternalInput")
    aup_r = TD("aup_r", [128, NDC * RLO], F32, kind="ExternalInput")
    bupl_r = TD("bupl_r", [32, RLO, 256], F32, kind="ExternalInput")
    wdn_r = TD("wdn_r", [NHC, 128, D], I8, kind="ExternalInput")
    sdn_r = TD("sdn_r", [NHC, 2, D], F32, kind="ExternalInput")
    bdn_row = TD("bdn_row", [1, D], F32, kind="ExternalInput")
    adn_r = TD("adn_r", [128, NHC * RLO], F32, kind="ExternalInput")
    bdnl_r = TD("bdnl_r", [RLO, D], F32, kind="ExternalInput")
    bt_in = TD("bt", [32, 2 * 32768], BF16, kind="ExternalInput")
    ident_in = TD("ident", [128, 128], BF16, kind="ExternalInput")
    ones_f_in = TD("ones_f", [1, 128], F32, kind="ExternalInput")
    ones_bf_in = TD("ones_bf", [1, 128], BF16, kind="ExternalInput")
    sel2_in = TD("sel2", [2, 128], BF16, kind="ExternalInput")
    vones_in = TD("vones", [128, 1040], BF16, kind="ExternalInput")
    epsc_in = TD("epsc", [128, 1], F32, kind="ExternalInput")
    out_t = TD("out", [TOK, D], F32, kind="ExternalOutput")
    t_dram = TD("t_dram", [NHL, 65536], BF16)
    rd_dram = TD("rd_dram", [32, 1024], F32)
    bar_src = TD("bar_src", [1, 32], F32)
    bar_dst = TD("bar_dst", [2, 32], F32)
    wupd_dram = TD("wupd_dram", [32, 128, 4096], BF16)
    av_loc = TD("av_loc", [TOK, S], BF16)
    av_g = TD("av_g", [S, S], BF16)
    xno_dram = TD("xno_dram", [128, NDC, 2048], BF16)
    wdnd_dram = TD("wdnd_dram", [128, 8 * NHC * 256], BF16)

    with ExitStack() as L0:
        E0 = L0.enter_context
        kb = KB(nc, L0)
        ident = E0(nc.sbuf_tensor("ident_sb", [128, 128], BF16))
        ones_f = E0(nc.sbuf_tensor("ones_f_sb", [1, 128], F32))
        ones_bf = E0(nc.sbuf_tensor("ones_bf_sb", [1, 128], BF16))
        sel2 = E0(nc.sbuf_tensor("sel2_sb", [2, 128], BF16))
        consts = E0(nc.sbuf_tensor("consts_sb", [128, 2 * NHL], F32))
        bupc_sb = E0(nc.sbuf_tensor("bupc_sb", [128, NHC], F32))
        epsb = E0(nc.sbuf_tensor("epsb", [128, 1], F32))

        with ExitStack() as PF:
            EF = PF.enter_context
            kt = EF(nc.sbuf_tensor("kt", [128, 8 * 2048], BF16))
            qt_sb = EF(nc.sbuf_tensor("qt_sb", [128, 8 * 2048], BF16))
            vt = EF(nc.sbuf_tensor("vt", [128, NQT * 1040], BF16))
            PX = ExitStack()
            xn_fm = PX.enter_context(nc.sbuf_tensor("xn_fm", [128, NDC * 2048], BF16))
            with ExitStack() as P:
                E = P.enter_context
                rel_sb = E(nc.sbuf_tensor("rel_sb", [32, NHL], BF16))
                crow_sb = E(nc.sbuf_tensor("crow_sb", [1, 2 * NHL], F32))
                btbuf = E(nc.sbuf_tensor("btbuf", [32, 2048], BF16))
                tstage = [E(nc.sbuf_tensor(f"tstage{i}", [NHL, 2048], BF16)) for i in range(2)]
                xt = [E(nc.sbuf_tensor(f"xt{i}", [128, D], F32)) for i in range(2)]
                xnt = E(nc.sbuf_tensor("xnt", [128, D], BF16))
                stats = E(nc.sbuf_tensor("stats_ln", [128, 32], F32))
                psTb = E(nc.psum_tensor("psTb", [NHL, 2048], F32))
                psC = E(nc.psum_tensor("psC", [128, 2 * NHL], F32))
                psTr = E(nc.psum_tensor("psTr", [128, 2048], BF16))
                NT = NQT
                with nc.Block() as blk:
                    @blk.sync
                    def _(s):
                        for dst, srct in ((ident, ident_in), (ones_f, ones_f_in),
                                          (ones_bf, ones_bf_in), (sel2, sel2_in),
                                          (crow_sb, crow), (bupc_sb, bup_c),
                                          (epsb, epsc_in)):
                            kb.d(s.dma_start(out=dst[:, :], in_=srct[:, :]), "c_ld")

                        def store(ch):
                            s.wait_ge(kb.sem("p0_ev"), 4 * (ch + 1))
                            ap2 = bass.AP(t_dram, ch * 2048, [[65536, NHL], [1, 2048]])
                            kb.d(s.dma_start(out=ap2, in_=tstage[ch % 2][:, :]),
                                 "p0_st%d" % (ch % 2))
                        for i in range(0, NT, 2):
                            if i >= 2:
                                s.wait_ge(kb.sem("p1_xn"), i - 1)
                            kb.d(s.dma_start(out=xt[0][:, :],
                                             in_=x_full[i * 128:(i + 1) * 128, :]), "p1_x0")
                            k = i // 2
                            if k >= 1:
                                for ch in range(4 * (k - 1), 4 * k):
                                    store(ch)
                        for ch in range(28, 32):
                            store(ch)
                        s.wait_ge(kb.sem("p0_st0"), kb.val["p0_st0"])
                        s.wait_ge(kb.sem("p0_st1"), kb.val["p0_st1"])

                    @blk.gpsimd
                    def _(g):
                        kb.d(g.dma_start(out=rel_sb[:, :], in_=rel_loc[:, :]), "g_rel")

                    @blk.tensor
                    def _(t):
                        t.wait_ge(kb.sem("g_rel"), 16)
                        for i in range(NT):
                            for ch in (2 * i, 2 * i + 1):
                                t.wait_ge(kb.sem("p0_bt"), 16 * (ch + 1))
                                if ch >= 1:
                                    t.wait_ge(kb.sem("p0_ev"), 4 * ch)
                                for k in range(4):
                                    kb.c(t.matmul(psTb[:, k * 512:(k + 1) * 512], rel_sb[:, :],
                                                  btbuf[:, k * 512:(k + 1) * 512],
                                                  start=True, stop=True), "p0_mm")
                            sl = i % 2
                            t.wait_ge(kb.sem("p1_xn"), i + 1)
                            for dc in range(NDC):
                                hb = dc // 8
                                if dc % 8 == 0 and i >= 1:
                                    t.wait_ge(kb.sem("p1_ev"), 2 * (i - 1) + hb + 1)
                                ins = t.transpose(
                                    psTr[:, hb * 1024 + (dc % 8) * 128: hb * 1024 + (dc % 8 + 1) * 128],
                                    xnt[:, dc * 128:(dc + 1) * 128], ident[:, :])
                                if dc % 8 == 7:
                                    kb.c(ins, "p1_tr")
                        t.wait_ge(kb.sem("c_ld"), 7 * 16)
                        kb.c(t.matmul(psC[:, :], ones_f[0:1, :], crow_sb[0:1, :],
                                      start=True, stop=True), "p0_mm")

                    @blk.vector
                    def _(v):
                        def evac(ch):
                            for k in range(4):
                                v.wait_ge(kb.sem("p0_mm"), 4 * ch + k + 1)
                                if ch >= 2 and k == 0:
                                    v.wait_ge(kb.sem("p0_st%d" % (ch % 2)), 16 * (ch // 2))
                                kb.c(v.tensor_copy(tstage[ch % 2][:, k * 512:(k + 1) * 512],
                                                   psTb[:, k * 512:(k + 1) * 512]), "p0_ev")
                        for i in range(NT):
                            if i >= 1:
                                evac(2 * (i - 1))
                                evac(2 * i - 1)
                            sl = i % 2
                            ba = 8 * (i % 4)
                            v.wait_ge(kb.sem("p1_x%d" % sl), 16 * (i // 2 + 1))
                            if i >= 4:
                                v.wait_ge(kb.sem("p1_xn"), i - 3)
                            kb.flush(v, v.reduce_sum(stats[:, ba:ba + 1], xt[sl][:, :], axis=AX.X))
                            kb.flush(v, v.tensor_scalar(stats[:, ba + 2:ba + 3], stats[:, ba:ba + 1],
                                                        1.0 / D, None, AluOpType.mult))
                            kb.flush(v, v.scalar_tensor_tensor(stats[:, ba + 3:ba + 4],
                                                               stats[:, ba + 2:ba + 3], -1.0,
                                                               stats[:, ba + 2:ba + 3],
                                                               AluOpType.mult, AluOpType.mult))
                            v.wait_ge(kb.sem("p1_sq"), i + 1)
                            kb.c(v.scalar_tensor_tensor(stats[:, ba + 4:ba + 5],
                                                        stats[:, ba + 1:ba + 2], 1.0 / D,
                                                        stats[:, ba + 3:ba + 4],
                                                        AluOpType.mult, AluOpType.add), "p1_s2")
                            v.wait_ge(kb.sem("p1_sqrt"), i + 1)
                            kb.flush(v, v.reciprocal(stats[:, ba + 7:ba + 8],
                                                     stats[:, ba + 5:ba + 6]))
                            kb.c(v.scalar_tensor_tensor(stats[:, ba + 6:ba + 7],
                                                        stats[:, ba + 2:ba + 3], -1.0,
                                                        stats[:, ba + 7:ba + 8],
                                                        AluOpType.mult, AluOpType.mult), "p1_s3")
                        evac(30)
                        evac(31)
                        v.wait_ge(kb.sem("p0_mm"), 4 * 32 + 1)
                        kb.c(v.tensor_copy(consts[:, :], psC[:, :]), "p0_cc")

                    @blk.scalar
                    def _(sc):
                        kb.d(sc.dma_start(out=xt[1][:, :], in_=x_full[128:256, :]), "p1_x1")
                        for i in range(NT):
                            sl = i % 2
                            ba = 8 * (i % 4)
                            for ch in (2 * i, 2 * i + 1):
                                if ch >= 1:
                                    sc.wait_ge(kb.sem("p0_mm"), 4 * ch)
                                ap = bass.AP(bt_in, ch * 2048, [[2 * 32768, 32], [1, 2048]])
                                kb.d(sc.dma_start(out=btbuf[:, :], in_=ap), "p0_bt")
                            sc.wait_ge(kb.sem("p1_x%d" % sl), 16 * (i // 2 + 1))
                            if i >= 4:
                                sc.wait_ge(kb.sem("p1_xn"), i - 3)
                            if i >= 1:
                                sc.wait_ge(kb.sem("p1_tr"), 2 * i)
                            kb.c(sc.activation(xnt[:, :], xt[sl][:, :], AF.Square,
                                               accum_out=stats[:, ba + 1:ba + 2]), "p1_sq")
                            if i == 0:
                                sc.wait_ge(kb.sem("c_ld"), 7 * 16)
                            sc.wait_ge(kb.sem("p1_s2"), i + 1)
                            kb.c(sc.activation(stats[:, ba + 5:ba + 6], stats[:, ba + 4:ba + 5],
                                               AF.Sqrt, bias=epsb[:, 0:1]), "p1_sqrt")
                            sc.wait_ge(kb.sem("p1_s3"), i + 1)
                            kb.c(sc.activation(xnt[:, :], xt[sl][:, :], AF.Identity,
                                               bias=stats[:, ba + 6:ba + 7],
                                               scale=stats[:, ba + 7:ba + 8]), "p1_xn")
                            if sl == 1 and i + 2 < NT:
                                sc.wait_ge(kb.sem("p1_xn"), kb.val["p1_xn"])
                                kb.d(sc.dma_start(out=xt[1][:, :],
                                                  in_=x_full[(i + 2) * 128:(i + 3) * 128, :]),
                                     "p1_x1")
                            if i >= 1:
                                _p1_evac(nc, kb, sc, i - 1, psTr, xn_fm)
                        _p1_evac(nc, kb, sc, NT - 1, psTr, xn_fm)

            _emit_p2(nc, kb, dict(
                kt=kt, qt=qt_sb, vt=vt, xn_fm=xn_fm, sel2=sel2,
                wk_r=wk_r, wq_r=wq_r, wv_r=wv_r, wdn_r=wdn_r, sdn_r=sdn_r,
                wdnd_dram=wdnd_dram, vones=vones_in, xno_dram=xno_dram,
                wup_r=wup_r, sup_r=sup_r, wupd_dram=wupd_dram))
            PX.close()

            _emit_p3(nc, kb, dict(
                kt=kt, qt=qt_sb, vt=vt, consts=consts, ident=ident,
                t_dram=t_dram, av_loc=av_loc, rd_dram=rd_dram))

        with ExitStack() as PF2:
            E5 = PF2.enter_context
            x2t = E5(nc.sbuf_tensor("x2t", [128, NHC * 1024], BF16))
            t2sb = E5(nc.sbuf_tensor("t2sb", [RLO, 1024], BF16))
            _emit_p5(nc, kb, dict(
                x2t=x2t, t2sb=t2sb, bupc=bupc_sb,
                xno_dram=xno_dram, wupd_dram=wupd_dram, bupl_r=bupl_r,
                aup_r=aup_r, adn_r=adn_r, av_loc=av_loc, av_g=av_g,
                bar_src=bar_src, bar_dst=bar_dst))
            _emit_p6(nc, kb, dict(
                x2t=x2t, t2sb=t2sb, ones_bf=ones_bf,
                wdnd_dram=wdnd_dram, av_g=av_g, wo_r=wo_r, bdn_row=bdn_row,
                bdnl_r=bdnl_r, x_own=x_own, out_t=out_t))
    return nc


def _p1_evac(nc, kb, sc, i, psTr, xn_fm):
    for hb in range(2):
        sc.wait_ge(kb.sem("p1_tr"), 2 * i + hb + 1)
        dst = bass.AP(xn_fm, hb * 8 * 2048 + i * 128,
                      [[NDC * 2048, 128], [2048, 8], [1, 128]])
        src = bass.AP(psTr, hb * 1024, [[2048, 128], [128, 8], [1, 128]])
        kb.c(sc.activation(dst, src, AF.Identity), "p1_ev")


def _emit_p2(nc, kb, tn):
    kt, qt, vt, xn_fm, sel2 = tn["kt"], tn["qt"], tn["vt"], tn["xn_fm"], tn["sel2"]
    wk_r, wq_r, wv_r = tn["wk_r"], tn["wq_r"], tn["wv_r"]
    wdn_r, sdn_r, wdnd_dram = tn["wdn_r"], tn["sdn_r"], tn["wdnd_dram"]
    vones, xno_dram_t = tn["vones"], tn["xno_dram"]
    wup_r, sup_r, wupd_dram = tn["wup_r"], tn["sup_r"], tn["wupd_dram"]

    def _q_src(item):
        if item < NHC:
            hcx = item
            return (wdn_r[hcx, :, :], 2048, sdn_r[hcx, :, :],
                    bass.AP(wdnd_dram, hcx * 256,
                            [[8 * NHC * 256, 128], [NHC * 256, 8], [1, 256]]))
        p = item - NHC
        hg, half = p // 2, p % 2
        return (bass.AP(wup_r, hg * (128 * 4096) + half * 2048, [[4096, 128], [1, 2048]]),
                2048,
                bass.AP(sup_r, hg * 8192 + half * 2048, [[4096, 2], [1, 2048]]),
                bass.AP(wupd_dram, hg * (128 * 4096) + half * 2048,
                        [[4096, 128], [1, 2048]]))
    with ExitStack() as P:
        E = P.enter_context
        wbuf = [E(nc.sbuf_tensor(f"wbuf{i}", [128, 2048], BF16)) for i in range(2)]
        wvbuf = E(nc.sbuf_tensor("wvbuf", [128, NDC * 256], BF16))
        stg = [E(nc.sbuf_tensor(f"stg{i}", [128, D], BF16)) for i in range(2)]
        sdsb = [E(nc.sbuf_tensor(f"sdsb{i}", [2, D], BF16)) for i in range(2)]
        wdd = [E(nc.sbuf_tensor(f"wdd{i}", [128, D], BF16)) for i in range(2)]
        psP = E(nc.psum_tensor("psP", [128, 2048], F32))
        psV = [E(nc.psum_tensor(f"psV{i}", [128, 512], F32)) for i in range(2)]
        psSc = [E(nc.psum_tensor(f"psSc{i}", [128, 512], F32)) for i in range(2)]
        with nc.Block() as blk:
            @blk.gpsimd
            def _(g):
                for kc in range(NQT):
                    kb.d(g.dma_start(out=vt[:, kc * 1040:(kc + 1) * 1040],
                                     in_=vones[:, :]), "p2_on")
                for w in range(16):
                    sl = w % 2
                    if w >= 2:
                        g.wait_ge(kb.sem("p2_proj"), w - 1)
                    wsrc = wk_r if w < 8 else wq_r
                    tt = w % 8
                    ap = bass.AP(wsrc, tt * (128 * NDC * 128), [[NDC * 128, 128], [1, NDC * 128]])
                    kb.d(g.dma_start(out=wbuf[sl][:, :], in_=ap, max_dma_last_dim=8192),
                         "p2_w%d" % sl)
                    for item in range(8 * w, 8 * w + 8):
                        hsl = item % 2
                        if item >= 2:
                            g.wait_ge(kb.sem("p2_deq"), 4 * (item - 1))
                        sap, mdld, scap, _ = _q_src(item)
                        kb.d(g.dma_start(out=stg[hsl][:, :], in_=sap,
                                         max_dma_last_dim=mdld), "p2_stg%d" % hsl)
                        kb.d(g.dma_start(out=sdsb[hsl][0:2, :], in_=scap,
                                         max_dma_last_dim=8192), "p2_stg%d" % hsl)
                for fq in range(4):
                    if fq >= 1:
                        g.wait_ge(kb.sem("p2_vkc"), 16 * fq)
                    ap = bass.AP(wv_r, fq * (128 * NDC * 256), [[NDC * 256, 128], [1, NDC * 256]])
                    kb.d(g.dma_start(out=wvbuf[:, :], in_=ap, max_dma_last_dim=8192), "p2_wv")

            @blk.sync
            def _(s):
                s.wait_ge(kb.sem("p1_ev"), 2 * NQT)
                kb.d(s.dma_start(out=xno_dram_t[:, :, :], in_=xn_fm[:, :]), "p1_pk")
                for hcx in range(NHC):
                    sl = hcx % 2
                    s.wait_ge(kb.sem("p2_deq"), 4 * (hcx + 1))
                    ap = bass.AP(wdnd_dram, hcx * 256,
                                 [[8 * NHC * 256, 128], [NHC * 256, 8], [1, 256]])
                    kb.d(s.dma_start(out=ap, in_=wdd[sl][:, :]), "p2_dst%d" % sl)
                if True:
                    s.wait_ge(kb.sem("p2_dst0"), kb.val["p2_dst0"])
                    s.wait_ge(kb.sem("p2_dst1"), kb.val["p2_dst1"])

            @blk.tensor
            def _(t):
                for w in range(16):
                    sl = w % 2
                    t.wait_ge(kb.sem("p2_w%d" % sl), 16 * (w // 2 + 1))
                    if w == 0:
                        t.wait_ge(kb.sem("p1_ev"), 2 * NQT)
                    if w >= 1:
                        t.wait_ge(kb.sem("p2_evp"), w)
                    for dc in range(NDC):
                        for bk in range(4):
                            ins = t.matmul(psP[:, bk * 512:(bk + 1) * 512],
                                           wbuf[sl][:, dc * 128:(dc + 1) * 128],
                                           xn_fm[:, dc * 2048 + bk * 512: dc * 2048 + (bk + 1) * 512],
                                           start=(dc == 0), stop=(dc == NDC - 1))
                    kb.c(ins, "p2_proj")
                    for item in range(8 * w, 8 * w + 8):
                        ssl = item % 2
                        t.wait_ge(kb.sem("p2_stg%d" % ssl), 32 * (item // 2 + 1))
                        for k in range(4):
                            m = 4 * item + k
                            if m >= 2:
                                t.wait_ge(kb.sem("p2_deq"), m - 1)
                            kb.c(t.matmul(psSc[m % 2][:, :], sel2[0:2, :],
                                          sdsb[ssl][0:2, k * 512:(k + 1) * 512],
                                          start=True, stop=True), "p2_scm")
                for fq in range(4):
                    t.wait_ge(kb.sem("p2_wv"), 16 * (fq + 1))
                    for kc in range(NQT):
                        vi = 16 * fq + kc
                        if vi >= 2:
                            t.wait_ge(kb.sem("p2_evv"), 4 * (vi - 1))
                        for dc in range(NDC):
                            ins = t.matmul(psV[vi % 2][:, 0:256],
                                           xn_fm[:, dc * 2048 + kc * 128: dc * 2048 + (kc + 1) * 128],
                                           wvbuf[:, dc * 256:(dc + 1) * 256],
                                           start=(dc == 0), stop=(dc == NDC - 1))
                        kb.c(ins, "p2_vkc")

            @blk.vector
            def _(v):
                for item in range(2 * NHC):
                    sl = item % 2
                    v.wait_ge(kb.sem("p2_stg%d" % sl), 32 * (item // 2 + 1))
                    if item >= 2:
                        prev = item - 2
                        nm = ("p2_dst%d" % (prev % 2)) if prev < NHC else \
                             ("p2_ust%d" % (prev % 2))
                        cnt = (prev // 2 + 1) if prev < NHC else ((prev - NHC) // 2 + 1)
                        v.wait_ge(kb.sem(nm), 16 * cnt)
                    for k in range(4):
                        v.wait_ge(kb.sem("p2_scm"), 4 * item + k + 1)
                        kb.c(v.scalar_tensor_tensor(wdd[sl][:, k * 512:(k + 1) * 512],
                                                    stg[sl][:, k * 512:(k + 1) * 512],
                                                    7.5, psSc[(4 * item + k) % 2][:, :],
                                                    AluOpType.subtract, AluOpType.mult), "p2_deq")

            @blk.scalar
            def _(sc):
                for w in range(16):
                    sc.wait_ge(kb.sem("p2_proj"), w + 1)
                    dst = kt if w < 8 else qt
                    kb.c(sc.activation(dst[:, (w % 8) * 2048:(w % 8 + 1) * 2048],
                                       psP[:, :], AF.Identity), "p2_evp")
                    if w >= 8:
                        for item in range(8 * w, 8 * w + 8):
                            sl = item % 2
                            sc.wait_ge(kb.sem("p2_deq"), 4 * (item + 1))
                            _, _, _, dap = _q_src(item)
                            kb.d(sc.dma_start(out=dap, in_=wdd[sl][:, :]),
                                 "p2_ust%d" % sl)
                for fq in range(4):
                    for kc in range(NQT):
                        vi = 16 * fq + kc
                        if vi == 0:
                            sc.wait_ge(kb.sem("p2_on"), 16 * NQT)
                        sc.wait_ge(kb.sem("p2_vkc"), vi + 1)
                        for j in range(4):
                            hcol = kc * 1040 + (fq * 4 + j) * 65
                            kb.c(sc.activation(vt[:, hcol:hcol + 64],
                                               psV[vi % 2][:, j * 64:(j + 1) * 64],
                                               AF.Identity), "p2_evv")
                sc.wait_ge(kb.sem("p2_ust0"), kb.val["p2_ust0"])
                sc.wait_ge(kb.sem("p2_ust1"), kb.val["p2_ust1"])


def _emit_p3(nc, kb, tn):
    kt_sb, qt, vt = tn["kt"], tn["qt"], tn["vt"]
    consts, ident = tn["consts"], tn["ident"]
    t_dram, av_loc, rd_dram = tn["t_dram"], tn["av_loc"], tn["rd_dram"]

    def regions(half, kt):
        W0, W1 = half * 1024, half * 1024 + 1024
        out = []
        for lo, hi, kind in ((0, kt * 128, "c0"),
                             (kt * 128, min(kt * 128 + 256, 2048), "var"),
                             (kt * 128 + 256, 2048, "c31")):
            l, h2 = max(lo, W0), min(hi, W1)
            if l < h2:
                out.append((l - W0, h2 - W0, kind))
        return out

    def bias_tiles(half, kt):
        tiles = []
        for b, sec in ((kt, 0), (kt + 1, 1)):
            if b <= 15 and 8 * half <= b < 8 * half + 8:
                tiles.append((b - 8 * half, sec))
        if len(tiles) == 2 and (tiles[0][0] * 128) % 512 != 384:
            return [(tiles[0][0] * 128, 256, 0)]
        return [(t[0] * 128, 128, t[1]) for t in tiles]

    with ExitStack() as P:
        E = P.enter_context
        bias_sb = E(nc.sbuf_tensor("bias_sb", [128, NHL * 256], BF16))
        PT = [E(nc.sbuf_tensor(f"pt{i}", [128, 1024], BF16)) for i in range(2)]
        attn_sb = [E(nc.sbuf_tensor(f"attn_sb{i}", [64, 2048], BF16)) for i in range(2)]
        rd_sb = E(nc.sbuf_tensor("rd_sb", [128, 2048], F32))
        rdraw = [E(nc.sbuf_tensor(f"rdraw{i}", [64, 1024], F32)) for i in range(2)]
        rdb = [E(nc.sbuf_tensor(f"rdb{i}", [64, 1024], F32)) for i in range(2)]
        psS2 = [E(nc.psum_tensor(f"psS2_{i}", [128, 1024], F32)) for i in range(2)]
        psAV = [E(nc.psum_tensor(f"psAV{i}", [65, 1024], F32)) for i in range(2)]
        with nc.Block() as blk:
            @blk.sync
            def _(s):
                for h in range(NHL):
                    for sec in range(2):
                        ap = bass.AP(t_dram, h * 65536 + sec * 32768, [[256, 128], [1, 128]])
                        kb.d(s.dma_start(out=bias_sb[:, h * 256 + sec * 128:
                                                     h * 256 + sec * 128 + 128],
                                         in_=ap), "p3_bias")
                for u in range(32):
                    h, half = u // 2, u % 2
                    s.wait_ge(kb.sem("p3_dnv"), u + 1)
                    kb.d(s.dma_start(out=rd_dram[u:u + 1, :],
                                     in_=rd_sb[64:65, half * 1024:half * 1024 + 1024]),
                         "p3_dw")
                    if u >= 2:
                        s.wait_ge(kb.sem("p3_rc"), u - 1)
                    s.wait_ge(kb.sem("p3_dw"), 16 * (u + 1))
                    ap = bass.AP(rd_dram, u * 1024, [[0, 64], [1, 1024]])
                    kb.d(s.dma_start(out=rdraw[u % 2][0:64, :], in_=ap), "p3_db")
                    if half == 1:
                        s.wait_ge(kb.sem("p3_nm"), u + 1)
                        ap2 = bass.AP(av_loc, h * 64 * S, [[S, 64], [1, S]])
                        kb.d(s.dma_start(out=ap2, in_=attn_sb[h % 2][0:64, :]), "p3_st")
                s.wait_ge(kb.sem("p3_st"), 16 * 16)

            def scores(t, g):
                u, ktn = g // 16, g % 16
                h, half = u // 2, u % 2
                hp, hs = h // 2, h % 2
                if g == 0:
                    t.wait_ge(kb.sem("p3_bias"), 16 * 32)
                    t.wait_ge(kb.sem("p2_evp"), 16)
                    t.wait_ge(kb.sem("p0_cc"), 1)
                if g >= 2:
                    t.wait_ge(kb.sem("p3_exp"), g - 1)
                qbase = hp * 2048 + half * 1024
                writes = [("sc", 0), ("sc", 1)]
                for (L, wdt, sec) in bias_tiles(half, ktn):
                    writes.append(("bias", L, wdt, sec))
                bank_of = lambda w: w[1] if w[0] == "sc" else w[1] // 512
                last = {}
                for idx, w in enumerate(writes):
                    last[bank_of(w)] = idx
                ins = None
                for idx, w in enumerate(writes):
                    stop = (last[bank_of(w)] == idx)
                    if w[0] == "sc":
                        bk = w[1]
                        ins = t.matmul(psS2[g % 2][:, bk * 512:(bk + 1) * 512],
                                       kt_sb[hs * 64:hs * 64 + 64,
                                             hp * 2048 + ktn * 128: hp * 2048 + (ktn + 1) * 128],
                                       qt[hs * 64:hs * 64 + 64,
                                          qbase + bk * 512: qbase + (bk + 1) * 512],
                                       start=True, stop=stop)
                    else:
                        _, L, wdt, sec = w
                        ins = t.matmul(psS2[g % 2][:, L:L + wdt], ident[:, :],
                                       bias_sb[:, h * 256 + sec * 128:
                                               h * 256 + sec * 128 + wdt],
                                       start=False, stop=stop)
                kb.c(ins, "p3_sc")

            def av(t, g):
                u, ktn = g // 16, g % 16
                h, half = u // 2, u % 2
                t.wait_ge(kb.sem("p3_exp"), g + 1)
                if g == 0:
                    t.wait_ge(kb.sem("p2_evv"), 256)
                    t.wait_ge(kb.sem("p2_on"), 16 * NQT)
                if ktn == 0 and u >= 2:
                    t.wait_ge(kb.sem("p3_nm"), u - 1)
                ins = None
                for bk in range(2):
                    ins = t.matmul(psAV[half][:, bk * 512:(bk + 1) * 512],
                                   vt[:, ktn * 1040 + h * 65: ktn * 1040 + h * 65 + 65],
                                   PT[g % 2][:, bk * 512:(bk + 1) * 512],
                                   start=(ktn == 0), stop=(ktn == 15))
                kb.c(ins, "p3_avk")

            @blk.tensor
            def _(t):
                for it in range(514):
                    if it < 512:
                        scores(t, it)
                    if it >= 2:
                        av(t, it - 2)

            @blk.scalar
            def _(sc):
                for g in range(512):
                    u, ktn = g // 16, g % 16
                    h, half = u // 2, u % 2
                    sc.wait_ge(kb.sem("p3_sc"), g + 1)
                    if g >= 2:
                        sc.wait_ge(kb.sem("p3_avk"), g - 1)
                    ins = None
                    for (lo, hi, kind) in regions(half, ktn):
                        if kind == "var":
                            ins = sc.activation(PT[g % 2][:, lo:hi],
                                                psS2[g % 2][:, lo:hi], AF.Exp)
                        else:
                            col = 2 * h if kind == "c0" else 2 * h + 1
                            ins = sc.activation(PT[g % 2][:, lo:hi],
                                                psS2[g % 2][:, lo:hi], AF.Exp,
                                                bias=consts[:, col:col + 1])
                    kb.c(ins, "p3_exp")

            @blk.vector
            def _(v):
                for u in range(32):
                    h, half = u // 2, u % 2
                    v.wait_ge(kb.sem("p3_avk"), 16 * (u + 1))
                    kb.c(v.tensor_copy(rd_sb[64:65, half * 1024:half * 1024 + 1024],
                                       psAV[half][64:65, :]), "p3_dnv")
                    v.wait_ge(kb.sem("p3_db"), 16 * (u + 1))
                    if u >= 2:
                        v.wait_ge(kb.sem("p3_nm"), u - 1)
                    kb.c(v.reciprocal(rdb[u % 2][0:64, :], rdraw[u % 2][0:64, :]), "p3_rc")
                    v.wait_ge(kb.sem("p3_rc"), kb.val["p3_rc"])
                    if half == 0 and h >= 2:
                        v.wait_ge(kb.sem("p3_st"), 16 * (h - 1))
                    kb.c(v.tensor_tensor(attn_sb[h % 2][0:64, half * 1024:half * 1024 + 1024],
                                         psAV[half][0:64, :], rdb[u % 2][0:64, :],
                                         AluOpType.mult), "p3_nm")


def _emit_p5(nc, kb, tn):
    x2t, t2sb, bupc = tn["x2t"], tn["t2sb"], tn["bupc"]
    xno_dram, wupd_dram = tn["xno_dram"], tn["wupd_dram"]
    bupl_r, aup_r, adn_r = tn["bupl_r"], tn["aup_r"], tn["adn_r"]
    av_loc, av_g = tn["av_loc"], tn["av_g"]
    bar_src, bar_dst = tn["bar_src"], tn["bar_dst"]
    with ExitStack() as P:
        E = P.enter_context
        xno = E(nc.sbuf_tensor("xno", [128, NDC * 1024], BF16))
        wud = [E(nc.sbuf_tensor(f"wud{i}", [128, NDC * 256], BF16)) for i in range(2)]
        buplb = [E(nc.sbuf_tensor(f"buplb{i}", [RLO, 256], BF16)) for i in range(2)]
        aupb = E(nc.sbuf_tensor("aupb", [128, NDC * RLO], BF16))
        adnb = E(nc.sbuf_tensor("adnb", [128, NHC * RLO], BF16))
        t1sb = E(nc.sbuf_tensor("t1sb", [RLO, 1024], BF16))
        psTl = [E(nc.psum_tensor(f"psTl{i}", [RLO, 512], F32)) for i in range(2)]
        psY1 = [E(nc.psum_tensor(f"psY1_{i}", [128, 1024], F32)) for i in range(2)]
        with nc.Block() as blk:
            @blk.gpsimd
            def _(g):
                kb.d(g.dma_start(out=aupb[:, :], in_=aup_r[:, :]), "p5_ld")
                kb.d(g.dma_start(out=adnb[:, :], in_=adn_r[:, :], max_dma_last_dim=4096), "p5_ld")

                def wload(hg):
                    sl = hg % 2
                    if hg >= 2:
                        g.wait_ge(kb.sem("p5_up"), 2 * (hg - 1))
                    g.wait_ge(kb.sem("p2_ust0"), 16 * (hg + 1))
                    g.wait_ge(kb.sem("p2_ust1"), 16 * (hg + 1))
                    kb.d(g.dma_start(out=wud[sl][:, :], in_=wupd_dram[hg, :, :]), "p5_stg")
                    ap = bass.AP(bupl_r, hg * (RLO * 256), [[256, RLO], [1, 256]])
                    kb.d(g.dma_start(out=buplb[sl][0:RLO, :], in_=ap), "p5_stg")
                wload(0)
                wload(1)
                g.wait_ge(kb.sem("p3_st"), 16 * 16)
                kb.c(g.collective_compute(
                    "AllGather", AluOpType.bypass,
                    replica_groups=[[2 * i, 2 * i + 1] for i in range(4)],
                    ins=[av_loc.ap().opt()],
                    outs=[av_g.ap().opt()]), "p5_cc")
                kb.c(g.collective_compute(
                    "AllGather", AluOpType.bypass,
                    replica_groups=[[2 * i, 2 * i + 1] for i in range(4)],
                    ins=[bar_src.ap().opt()],
                    outs=[bar_dst.ap().opt()]), "p5_cc")
                for hg in range(2, 32):
                    wload(hg)

            @blk.sync
            def _(s):
                s.wait_ge(kb.sem("p1_pk"), 16)
                pid = s.partition_id()
                hoff = s.compute_val((pid % 2) * 1024)
                kb.d(s.dma_start(out=xno[:, :],
                                 in_=xno_dram[:, :, bass.ds(hoff, 1024)]), "p5_xno")

            @blk.tensor
            def _(t):
                t.wait_ge(kb.sem("p5_xno"), 16)
                t.wait_ge(kb.sem("p5_ld"), 32)
                for half in range(2):
                    for dc in range(NDC):
                        ins = t.matmul(psTl[half][:, :], aupb[:, dc * RLO:(dc + 1) * RLO],
                                       xno[:, dc * 1024 + half * 512: dc * 1024 + (half + 1) * 512],
                                       start=(dc == 0), stop=(dc == NDC - 1))
                    kb.c(ins, "p5_t1")
                for hg in range(32):
                    sl = hg % 2
                    t.wait_ge(kb.sem("p5_stg"), 32 * (hg + 1))
                    for hc2 in range(2):
                        hcx = 2 * hg + hc2
                        if hcx >= 2:
                            t.wait_ge(kb.sem("p5_rel"), hcx - 1)
                        if hcx == 0:
                            t.wait_ge(kb.sem("p5_t1e"), 2)
                        for half in range(2):
                            for dc in range(NDC):
                                t.matmul(psY1[hcx % 2][:, half * 512:(half + 1) * 512],
                                         wud[sl][:, dc * 256 + hc2 * 128: dc * 256 + hc2 * 128 + 128],
                                         xno[:, dc * 1024 + half * 512: dc * 1024 + (half + 1) * 512],
                                         start=(dc == 0), stop=False)
                            ins = t.matmul(psY1[hcx % 2][:, half * 512:(half + 1) * 512],
                                           buplb[sl][0:RLO, hc2 * 128: hc2 * 128 + 128],
                                           t1sb[0:RLO, half * 512:(half + 1) * 512],
                                           start=False, stop=True)
                        kb.c(ins, "p5_up")
                        if hcx >= 1:
                            _p5_t2mm(nc, kb, t, hcx - 1, psTl, adnb, x2t)
                _p5_t2mm(nc, kb, t, NHC - 1, psTl, adnb, x2t)

            @blk.vector
            def _(v):
                for half in range(2):
                    v.wait_ge(kb.sem("p5_t1"), half + 1)
                    kb.c(v.tensor_copy(t1sb[0:RLO, half * 512:(half + 1) * 512],
                                       psTl[half][:, :]), "p5_t1e")
                v.wait_ge(kb.sem("p5_t2"), NHC)
                for half in range(2):
                    kb.c(v.tensor_copy(t2sb[0:RLO, half * 512:(half + 1) * 512],
                                       psTl[half][:, :]), "p5_t2e")

            @blk.scalar
            def _(sc):
                for hcx in range(NHC):
                    sc.wait_ge(kb.sem("p5_up"), hcx + 1)
                    kb.c(sc.activation(x2t[:, hcx * 1024:(hcx + 1) * 1024],
                                       psY1[hcx % 2][:, :], AF.Relu,
                                       bias=bupc[:, hcx:hcx + 1]), "p5_rel")


def _p5_t2mm(nc, kb, t, hcx, psTl, adnb, x2t):
    t.wait_ge(kb.sem("p5_rel"), hcx + 1)
    if hcx == 0:
        t.wait_ge(kb.sem("p5_t1e"), 2)
    for half in range(2):
        ins = t.matmul(psTl[half][:, :], adnb[:, hcx * RLO:(hcx + 1) * RLO],
                       x2t[:, hcx * 1024 + half * 512: hcx * 1024 + (half + 1) * 512],
                       start=(hcx == 0), stop=(hcx == NHC - 1))
    kb.c(ins, "p5_t2")


def _emit_p6(nc, kb, tn):
    x2t, t2sb, ones_bf = tn["x2t"], tn["t2sb"], tn["ones_bf"]
    wdnd_dram, av_g, wo_r = tn["wdnd_dram"], tn["av_g"], tn["wo_r"]
    bdn_row, bdnl_r, x_own, out_t = tn["bdn_row"], tn["bdnl_r"], tn["x_own"], tn["out_t"]
    with ExitStack() as P:
        E = P.enter_context
        wdnsb = [E(nc.sbuf_tensor(f"wdnsb{i}", [128, 32 * 256], BF16)) for i in range(2)]
        avb = [E(nc.sbuf_tensor(f"avb{i}", [128, NDC * 128], BF16)) for i in range(2)]
        wob = [E(nc.sbuf_tensor(f"wob{i}", [128, NDC * 256], BF16)) for i in range(2)]
        bdnb = E(nc.sbuf_tensor("bdnb", [1, D], BF16))
        bdnlb = E(nc.sbuf_tensor("bdnlb", [RLO, D], BF16))
        xrb = [E(nc.sbuf_tensor(f"xrb{i}", [128, 256], F32)) for i in range(2)]
        outb = [E(nc.sbuf_tensor(f"outb{i}", [128, 256], F32)) for i in range(2)]
        psY = [E(nc.psum_tensor(f"psY{i}", [128, 512], F32)) for i in range(2)]
        with nc.Block() as blk:
            @blk.sync
            def _(s):
                s.wait_ge(kb.sem("p5_cc"), 2)
                pid = s.partition_id()
                hoff = s.compute_val((pid % 2) * 1024)
                coffs = [s.compute_val(hoff + tt * 128) for tt in range(8)]
                for dg in range(8):
                    for tt in range(8):
                        gi = dg * 8 + tt
                        if gi >= 2:
                            s.wait_ge(kb.sem("p6_mm"), gi - 1)
                        ap = av_g[:, bass.ds(coffs[tt], 128)].rearrange(
                            "(dc p) j -> p dc j", p=128)
                        kb.d(s.dma_start(out=avb[gi % 2][:, :], in_=ap),
                             "p6_av%d" % (gi % 2))

            @blk.scalar
            def _(sc):
                def wdn_load(idx):
                    dg, hx = idx // 2, idx % 2
                    if dg >= 1:
                        sc.wait_ge(kb.sem("p6_mmh"), 16 * dg + hx - 1)
                    ap = bass.AP(wdnd_dram, dg * NHC * 256 + hx * 32 * 256,
                                 [[8 * NHC * 256, 128], [1, 32 * 256]])
                    kb.d(sc.dma_start(out=wdnsb[idx % 2][:, :], in_=ap), "p6_wdnh")
                wdn_load(0)
                wdn_load(1)
                for idx in range(2, 16):
                    wdn_load(idx)
                sc.wait_ge(kb.sem("p6_wdnh"), 16 * 16)

            @blk.gpsimd
            def _(g):
                kb.d(g.dma_start(out=bdnb[0:1, :], in_=bdn_row[:, :]), "p6_ld")
                kb.d(g.dma_start(out=bdnlb[0:RLO, :], in_=bdnl_r[:, :]), "p6_ld")
                for dg in range(2):
                    sl = dg % 2
                    ap = bass.AP(wo_r, dg * (128 * NDC * 256), [[NDC * 256, 128], [1, NDC * 256]])
                    kb.d(g.dma_start(out=wob[sl][:, :], in_=ap, max_dma_last_dim=8192),
                         "p6_wo%d" % sl)
                for gi in range(2):
                    dg, tt = gi // 8, gi % 8
                    kb.d(g.dma_start(out=xrb[gi % 2][:, :],
                                     in_=x_own[tt * 128:(tt + 1) * 128,
                                               dg * 256:(dg + 1) * 256]), "p6_xr%d" % (gi % 2))
                for gi in range(64):
                    dg, tt = gi // 8, gi % 8
                    g.wait_ge(kb.sem("p6_out"), gi + 1)
                    kb.d(g.dma_start(out=out_t[tt * 128:(tt + 1) * 128,
                                              dg * 256:(dg + 1) * 256],
                                     in_=outb[gi % 2][:, :]), "p6_st%d" % (gi % 2))
                    if gi + 2 < 64:
                        dg2, tt2 = (gi + 2) // 8, (gi + 2) % 8
                        kb.d(g.dma_start(out=xrb[gi % 2][:, :],
                                         in_=x_own[tt2 * 128:(tt2 + 1) * 128,
                                                   dg2 * 256:(dg2 + 1) * 256]),
                             "p6_xr%d" % (gi % 2))
                    if tt == 7 and dg + 2 < 8:
                        sl = dg % 2
                        ap = bass.AP(wo_r, (dg + 2) * (128 * NDC * 256),
                                     [[NDC * 256, 128], [1, NDC * 256]])
                        kb.d(g.dma_start(out=wob[sl][:, :], in_=ap, max_dma_last_dim=8192),
                             "p6_wo%d" % sl)
                g.wait_ge(kb.sem("p6_st0"), 16 * 32)
                g.wait_ge(kb.sem("p6_st1"), 16 * 32)

            @blk.tensor
            def _(t):
                t.wait_ge(kb.sem("p6_ld"), 32)
                for dg in range(8):
                    sl = dg % 2
                    t.wait_ge(kb.sem("p6_wo%d" % sl), 16 * (dg // 2 + 1))
                    for tt in range(8):
                        gi = dg * 8 + tt
                        t.wait_ge(kb.sem("p6_av%d" % (gi % 2)), 16 * (gi // 2 + 1))
                        if gi >= 2:
                            t.wait_ge(kb.sem("p6_out"), gi - 1)
                        if tt == 0:
                            t.wait_ge(kb.sem("p6_wdnh"), 16 * (2 * dg + 2))
                        for dc in range(NDC):
                            t.matmul(psY[gi % 2][:, 0:256], avb[gi % 2][:, dc * 128:(dc + 1) * 128],
                                     wob[sl][:, dc * 256:(dc + 1) * 256],
                                     start=(dc == 0), stop=False)
                        for hx in range(2):
                            for hc2 in range(32):
                                hcx = hx * 32 + hc2
                                ins = t.matmul(psY[gi % 2][:, 0:256],
                                               x2t[:, hcx * 1024 + tt * 128: hcx * 1024 + tt * 128 + 128],
                                               wdnsb[(2 * dg + hx) % 2][:, hc2 * 256:(hc2 + 1) * 256],
                                               start=False, stop=False)
                            kb.c(ins, "p6_mmh")
                        t.matmul(psY[gi % 2][:, 0:256], ones_bf[0:1, :],
                                 bdnb[0:1, dg * 256:(dg + 1) * 256], start=False, stop=False)
                        kb.c(t.matmul(psY[gi % 2][:, 0:256], t2sb[0:RLO, tt * 128:(tt + 1) * 128],
                                      bdnlb[0:RLO, dg * 256:(dg + 1) * 256],
                                      start=False, stop=True), "p6_mm")

            @blk.vector
            def _(v):
                for gi in range(64):
                    v.wait_ge(kb.sem("p6_mm"), gi + 1)
                    v.wait_ge(kb.sem("p6_xr%d" % (gi % 2)), 16 * (gi // 2 + 1))
                    if gi >= 2:
                        v.wait_ge(kb.sem("p6_st%d" % (gi % 2)), 16 * (gi // 2))
                    kb.c(v.tensor_tensor(outb[gi % 2][:, :], psY[gi % 2][:, 0:256],
                                         xrb[gi % 2][:, :], AluOpType.add), "p6_out")


def kernel(**inputs):
    maps = _per_core_inputs(inputs)
    nc = build_program()
    res = run_bass_kernel_spmd(nc, maps, core_ids=list(range(8)))
    out = np.empty((B, S, D), dtype=np.float32)
    for c in range(8):
        b, H = c // 2, c % 2
        out[b, H * 1024:(H + 1) * 1024, :] = res.results[c]["out"]
    return out


# version 18
# speedup vs baseline: 1.0187x; 1.0187x over previous
import math
import numpy as np
import ml_dtypes
import concourse.bass as bass
import concourse.mybir as mybir
from concourse.bass_utils import run_bass_kernel_spmd
from concourse.alu_op_type import AluOpType
from contextlib import ExitStack

F32 = mybir.dt.float32
BF16 = mybir.dt.bfloat16
I8 = mybir.dt.int8
AF = mybir.ActivationFunctionType
AX = mybir.AxisListType

B, S, D = 4, 2048, 2048
NHL = 16
NHP = NHL // 2
NQT = S // 128
NDC = D // 128
DFF = 8192
NHC = DFF // 128
RLO = 16
TOK = 1024
BF = ml_dtypes.bfloat16


def _bucket(d):
    rp = max(d, 0)
    if rp < 16:
        return rp
    v = 16 + int(np.float32(np.log(np.float32(rp) / 16.0)) / np.float32(math.log(8.0)) * 16)
    return min(v, 31)


_HOST_CONST_CACHE = {}


def _vones():
    v = np.zeros((128, 1040), dtype=BF)
    v[:, 64::65] = 1
    return v


def _host_consts():
    if "bt" in _HOST_CONST_CACHE:
        return _HOST_CONST_CACHE
    bt = np.zeros((32, 2 * 32768), dtype=BF)
    for sec, dlt in enumerate((0, 128)):
        base = sec * 32768
        for i in range(128):
            row = base + i * 256
            for j in range(128):
                bt[_bucket(dlt + j - i), row + j] = 1
    _HOST_CONST_CACHE.update(
        bt=bt,
        ident=np.eye(128, dtype=BF),
        ones_f=np.ones((1, 128), dtype=np.float32),
        ones_bf=np.ones((1, 128), dtype=BF),
        sel2=np.concatenate([
            np.concatenate([np.ones((1, 64), BF), np.zeros((1, 64), BF)], axis=1),
            np.concatenate([np.zeros((1, 64), BF), np.ones((1, 64), BF)], axis=1)], axis=0),
        vones=_vones(),
        epsc=np.full((128, 1), 1e-5, dtype=np.float32),
    )
    return _HOST_CONST_CACHE


def _per_core_inputs(inp):
    x = np.ascontiguousarray(np.asarray(inp["L0_attn_hidden_states"], dtype=np.float32))
    Wq = np.asarray(inp["L0_attn_Wq"], dtype=np.float32)
    Wk = np.asarray(inp["L0_attn_Wk"], dtype=np.float32)
    Wv = np.asarray(inp["L0_attn_Wv"], dtype=np.float32)
    Wo = np.asarray(inp["L0_attn_Wo"], dtype=np.float32)
    relb = np.asarray(inp["L0_attn_rel_bias"], dtype=np.float32)
    wupq = np.ascontiguousarray(np.asarray(inp["L0_ffn_w_up_q"]).view(np.int8).reshape(DFF, D, 4)[:, :, 0])
    sup = np.asarray(inp["L0_ffn_w_up_scale"], dtype=np.float32)
    bup = np.asarray(inp["L0_ffn_b_up"], dtype=np.float32)
    aup = np.asarray(inp["L0_ffn_w_up_lora_a"], dtype=np.float32)
    bupl = np.asarray(inp["L0_ffn_w_up_lora_b"], dtype=np.float32)
    wdnq = np.ascontiguousarray(np.asarray(inp["L0_ffn_w_down_q"]).view(np.int8).reshape(D, DFF, 4)[:, :, 0])
    sdn = np.asarray(inp["L0_ffn_w_down_scale"], dtype=np.float32)
    bdn = np.asarray(inp["L0_ffn_b_down"], dtype=np.float32)
    adn = np.asarray(inp["L0_ffn_w_down_lora_a"], dtype=np.float32)
    bdnl = np.asarray(inp["L0_ffn_w_down_lora_b"], dtype=np.float32)
    hc = _host_consts()

    wo_r = np.ascontiguousarray(Wo.reshape(NDC, 128, 8, 256).transpose(2, 1, 0, 3))
    wup_r = np.ascontiguousarray(wupq.T.reshape(NDC, 128, 32, 256).transpose(2, 1, 0, 3))
    sup_r = np.ascontiguousarray(sup.T.reshape(NDC, 2, 32, 256).transpose(2, 1, 0, 3))
    wdn_r = np.ascontiguousarray(wdnq.T.reshape(NHC, 128, D))
    sdn_r = np.ascontiguousarray(sdn.T.reshape(NHC, 2, D))
    bup_c = np.ascontiguousarray(bup.reshape(NHC, 128).T)
    aup_r = np.ascontiguousarray(aup.reshape(NDC, 128, RLO).transpose(1, 0, 2).reshape(128, NDC * RLO))
    bupl_r = np.ascontiguousarray(bupl.reshape(RLO, 32, 256).transpose(1, 0, 2))
    adn_r = np.ascontiguousarray(adn.reshape(NHC, 128, RLO).transpose(1, 0, 2).reshape(128, NHC * RLO))
    bdn_row = np.ascontiguousarray(bdn.reshape(1, D))
    bdnl_r = np.ascontiguousarray(bdnl)

    maps = []
    for c in range(8):
        b, H = c // 2, c % 2
        cols = slice(H * 1024, (H + 1) * 1024)
        heads = slice(H * NHL, (H + 1) * NHL)
        crow = np.empty((1, 2 * NHL), dtype=np.float32)
        crow[0, 0::2] = relb[0, heads]
        crow[0, 1::2] = relb[31, heads]
        maps.append({
            "x_full": x[b],
            "x_own": np.ascontiguousarray(x[b, H * 1024:(H + 1) * 1024]),
            "wq_r": np.ascontiguousarray(Wq[:, cols].reshape(NDC, 128, 8, 128).transpose(2, 1, 0, 3)),
            "wk_r": np.ascontiguousarray(Wk[:, cols].reshape(NDC, 128, 8, 128).transpose(2, 1, 0, 3)),
            "wv_r": np.ascontiguousarray(Wv[:, cols].reshape(NDC, 128, 4, 256).transpose(2, 1, 0, 3)),
            "wo_r": wo_r,
            "rel_loc": np.ascontiguousarray(relb[:, heads]),
            "crow": crow,
            "wup_r": wup_r, "sup_r": sup_r, "bup_c": bup_c,
            "aup_r": aup_r, "bupl_r": bupl_r,
            "wdn_r": wdn_r, "sdn_r": sdn_r, "bdn_row": bdn_row,
            "adn_r": adn_r, "bdnl_r": bdnl_r,
            "bt": hc["bt"], "ident": hc["ident"], "ones_f": hc["ones_f"],
            "ones_bf": hc["ones_bf"], "sel2": hc["sel2"], "vones": hc["vones"],
            "epsc": hc["epsc"],
        })
    return maps


class KB:
    def __init__(self, nc, stack):
        self.nc = nc
        self.stack = stack
        self.sems = {}
        self.val = {}

    def sem(self, name):
        if name not in self.sems:
            self.sems[name] = self.stack.enter_context(self.nc.semaphore(name))
            self.val[name] = 0
        return self.sems[name]

    def d(self, instr, name):
        s = self.sem(name)
        self.val[name] += 16
        instr.then_inc(s, 16)

    def c(self, instr, name):
        s = self.sem(name)
        self.val[name] += 1
        instr.then_inc(s, 1)

    def flush(self, eng, instr, name="slf"):
        self.c(instr, name)
        eng.wait_ge(self.sem(name), self.val[name])


def build_program():
    nc = bass.Bass("TRN2", target_bir_lowering=False)
    TD = nc.dram_tensor
    x_full = TD("x_full", [S, D], F32, kind="ExternalInput")
    x_own = TD("x_own", [TOK, D], F32, kind="ExternalInput")
    wq_r = TD("wq_r", [8, 128, NDC, 128], F32, kind="ExternalInput")
    wk_r = TD("wk_r", [8, 128, NDC, 128], F32, kind="ExternalInput")
    wv_r = TD("wv_r", [4, 128, NDC, 256], F32, kind="ExternalInput")
    wo_r = TD("wo_r", [8, 128, NDC, 256], F32, kind="ExternalInput")
    rel_loc = TD("rel_loc", [32, NHL], F32, kind="ExternalInput")
    crow = TD("crow", [1, 2 * NHL], F32, kind="ExternalInput")
    wup_r = TD("wup_r", [32, 128, NDC, 256], I8, kind="ExternalInput")
    sup_r = TD("sup_r", [32, 2, NDC, 256], F32, kind="ExternalInput")
    bup_c = TD("bup_c", [128, NHC], F32, kind="ExternalInput")
    aup_r = TD("aup_r", [128, NDC * RLO], F32, kind="ExternalInput")
    bupl_r = TD("bupl_r", [32, RLO, 256], F32, kind="ExternalInput")
    wdn_r = TD("wdn_r", [NHC, 128, D], I8, kind="ExternalInput")
    sdn_r = TD("sdn_r", [NHC, 2, D], F32, kind="ExternalInput")
    bdn_row = TD("bdn_row", [1, D], F32, kind="ExternalInput")
    adn_r = TD("adn_r", [128, NHC * RLO], F32, kind="ExternalInput")
    bdnl_r = TD("bdnl_r", [RLO, D], F32, kind="ExternalInput")
    bt_in = TD("bt", [32, 2 * 32768], BF16, kind="ExternalInput")
    ident_in = TD("ident", [128, 128], BF16, kind="ExternalInput")
    ones_f_in = TD("ones_f", [1, 128], F32, kind="ExternalInput")
    ones_bf_in = TD("ones_bf", [1, 128], BF16, kind="ExternalInput")
    sel2_in = TD("sel2", [2, 128], BF16, kind="ExternalInput")
    vones_in = TD("vones", [128, 1040], BF16, kind="ExternalInput")
    epsc_in = TD("epsc", [128, 1], F32, kind="ExternalInput")
    out_t = TD("out", [TOK, D], F32, kind="ExternalOutput")
    t_dram = TD("t_dram", [NHL, 65536], BF16)
    rd_dram = TD("rd_dram", [32, 1024], F32)
    bar_src = TD("bar_src", [1, 32], F32)
    bar_dst = TD("bar_dst", [2, 32], F32)
    wupd_dram = TD("wupd_dram", [32, 128, 4096], BF16)
    av_loc = TD("av_loc", [TOK, S], BF16)
    av_g = TD("av_g", [S, S], BF16)
    xno_dram = TD("xno_dram", [128, NDC, 2048], BF16)
    wdnd_dram = TD("wdnd_dram", [128, 8 * NHC * 256], BF16)

    with ExitStack() as L0:
        E0 = L0.enter_context
        kb = KB(nc, L0)
        ident = E0(nc.sbuf_tensor("ident_sb", [128, 128], BF16))
        ones_f = E0(nc.sbuf_tensor("ones_f_sb", [1, 128], F32))
        ones_bf = E0(nc.sbuf_tensor("ones_bf_sb", [1, 128], BF16))
        sel2 = E0(nc.sbuf_tensor("sel2_sb", [2, 128], BF16))
        consts = E0(nc.sbuf_tensor("consts_sb", [128, 2 * NHL], F32))
        bupc_sb = E0(nc.sbuf_tensor("bupc_sb", [128, NHC], F32))
        epsb = E0(nc.sbuf_tensor("epsb", [128, 1], F32))

        with ExitStack() as PF:
            EF = PF.enter_context
            kt = EF(nc.sbuf_tensor("kt", [128, 8 * 2048], BF16))
            qt_sb = EF(nc.sbuf_tensor("qt_sb", [128, 8 * 2048], BF16))
            vt = EF(nc.sbuf_tensor("vt", [128, NQT * 1040], BF16))
            PX = ExitStack()
            xn_fm = PX.enter_context(nc.sbuf_tensor("xn_fm", [128, NDC * 2048], BF16))
            with ExitStack() as P:
                E = P.enter_context
                rel_sb = E(nc.sbuf_tensor("rel_sb", [32, NHL], BF16))
                crow_sb = E(nc.sbuf_tensor("crow_sb", [1, 2 * NHL], F32))
                btbuf = E(nc.sbuf_tensor("btbuf", [32, 2048], BF16))
                tstage = [E(nc.sbuf_tensor(f"tstage{i}", [NHL, 2048], BF16)) for i in range(2)]
                xt = [E(nc.sbuf_tensor(f"xt{i}", [128, D], F32)) for i in range(2)]
                xnt = E(nc.sbuf_tensor("xnt", [128, D], BF16))
                stats = E(nc.sbuf_tensor("stats_ln", [128, 32], F32))
                psTb = E(nc.psum_tensor("psTb", [NHL, 2048], F32))
                psC = E(nc.psum_tensor("psC", [128, 2 * NHL], F32))
                psTr = E(nc.psum_tensor("psTr", [128, 2048], BF16))
                NT = NQT
                with nc.Block() as blk:
                    @blk.sync
                    def _(s):
                        for dst, srct in ((ident, ident_in), (ones_f, ones_f_in),
                                          (ones_bf, ones_bf_in), (sel2, sel2_in),
                                          (crow_sb, crow), (bupc_sb, bup_c),
                                          (epsb, epsc_in)):
                            kb.d(s.dma_start(out=dst[:, :], in_=srct[:, :]), "c_ld")

                        def store(ch):
                            s.wait_ge(kb.sem("p0_ev"), 4 * (ch + 1))
                            ap2 = bass.AP(t_dram, ch * 2048, [[65536, NHL], [1, 2048]])
                            kb.d(s.dma_start(out=ap2, in_=tstage[ch % 2][:, :]),
                                 "p0_st%d" % (ch % 2))
                        for i in range(0, NT, 2):
                            if i >= 2:
                                s.wait_ge(kb.sem("p1_xn"), i - 1)
                            kb.d(s.dma_start(out=xt[0][:, :],
                                             in_=x_full[i * 128:(i + 1) * 128, :]), "p1_x0")
                            k = i // 2
                            if k >= 1:
                                for ch in range(4 * (k - 1), 4 * k):
                                    store(ch)
                        for ch in range(28, 32):
                            store(ch)
                        s.wait_ge(kb.sem("p0_st0"), kb.val["p0_st0"])
                        s.wait_ge(kb.sem("p0_st1"), kb.val["p0_st1"])

                    @blk.gpsimd
                    def _(g):
                        kb.d(g.dma_start(out=rel_sb[:, :], in_=rel_loc[:, :]), "g_rel")

                    @blk.tensor
                    def _(t):
                        t.wait_ge(kb.sem("g_rel"), 16)
                        for i in range(NT):
                            for ch in (2 * i, 2 * i + 1):
                                t.wait_ge(kb.sem("p0_bt"), 16 * (ch + 1))
                                if ch >= 1:
                                    t.wait_ge(kb.sem("p0_ev"), 4 * ch)
                                for k in range(4):
                                    kb.c(t.matmul(psTb[:, k * 512:(k + 1) * 512], rel_sb[:, :],
                                                  btbuf[:, k * 512:(k + 1) * 512],
                                                  start=True, stop=True), "p0_mm")
                            sl = i % 2
                            t.wait_ge(kb.sem("p1_xn"), i + 1)
                            for dc in range(NDC):
                                hb = dc // 8
                                if dc % 8 == 0 and i >= 1:
                                    t.wait_ge(kb.sem("p1_ev"), 2 * (i - 1) + hb + 1)
                                ins = t.transpose(
                                    psTr[:, hb * 1024 + (dc % 8) * 128: hb * 1024 + (dc % 8 + 1) * 128],
                                    xnt[:, dc * 128:(dc + 1) * 128], ident[:, :])
                                if dc % 8 == 7:
                                    kb.c(ins, "p1_tr")
                        t.wait_ge(kb.sem("c_ld"), 7 * 16)
                        kb.c(t.matmul(psC[:, :], ones_f[0:1, :], crow_sb[0:1, :],
                                      start=True, stop=True), "p0_mm")

                    @blk.vector
                    def _(v):
                        def evac(ch):
                            for k in range(4):
                                v.wait_ge(kb.sem("p0_mm"), 4 * ch + k + 1)
                                if ch >= 2 and k == 0:
                                    v.wait_ge(kb.sem("p0_st%d" % (ch % 2)), 16 * (ch // 2))
                                kb.c(v.tensor_copy(tstage[ch % 2][:, k * 512:(k + 1) * 512],
                                                   psTb[:, k * 512:(k + 1) * 512]), "p0_ev")
                        for i in range(NT):
                            if i >= 1:
                                evac(2 * (i - 1))
                                evac(2 * i - 1)
                            sl = i % 2
                            ba = 8 * (i % 4)
                            v.wait_ge(kb.sem("p1_x%d" % sl), 16 * (i // 2 + 1))
                            if i >= 4:
                                v.wait_ge(kb.sem("p1_xn"), i - 3)
                            kb.flush(v, v.reduce_sum(stats[:, ba:ba + 1], xt[sl][:, :], axis=AX.X))
                            kb.flush(v, v.tensor_scalar(stats[:, ba + 2:ba + 3], stats[:, ba:ba + 1],
                                                        1.0 / D, None, AluOpType.mult))
                            kb.flush(v, v.scalar_tensor_tensor(stats[:, ba + 3:ba + 4],
                                                               stats[:, ba + 2:ba + 3], -1.0,
                                                               stats[:, ba + 2:ba + 3],
                                                               AluOpType.mult, AluOpType.mult))
                            v.wait_ge(kb.sem("p1_sq"), i + 1)
                            kb.c(v.scalar_tensor_tensor(stats[:, ba + 4:ba + 5],
                                                        stats[:, ba + 1:ba + 2], 1.0 / D,
                                                        stats[:, ba + 3:ba + 4],
                                                        AluOpType.mult, AluOpType.add), "p1_s2")
                            v.wait_ge(kb.sem("p1_sqrt"), i + 1)
                            kb.flush(v, v.reciprocal(stats[:, ba + 7:ba + 8],
                                                     stats[:, ba + 5:ba + 6]))
                            kb.c(v.scalar_tensor_tensor(stats[:, ba + 6:ba + 7],
                                                        stats[:, ba + 2:ba + 3], -1.0,
                                                        stats[:, ba + 7:ba + 8],
                                                        AluOpType.mult, AluOpType.mult), "p1_s3")
                        evac(30)
                        evac(31)
                        v.wait_ge(kb.sem("p0_mm"), 4 * 32 + 1)
                        kb.c(v.tensor_copy(consts[:, :], psC[:, :]), "p0_cc")

                    @blk.scalar
                    def _(sc):
                        kb.d(sc.dma_start(out=xt[1][:, :], in_=x_full[128:256, :]), "p1_x1")
                        for i in range(NT):
                            sl = i % 2
                            ba = 8 * (i % 4)
                            for ch in (2 * i, 2 * i + 1):
                                if ch >= 1:
                                    sc.wait_ge(kb.sem("p0_mm"), 4 * ch)
                                ap = bass.AP(bt_in, ch * 2048, [[2 * 32768, 32], [1, 2048]])
                                kb.d(sc.dma_start(out=btbuf[:, :], in_=ap), "p0_bt")
                            sc.wait_ge(kb.sem("p1_x%d" % sl), 16 * (i // 2 + 1))
                            if i >= 4:
                                sc.wait_ge(kb.sem("p1_xn"), i - 3)
                            if i >= 1:
                                sc.wait_ge(kb.sem("p1_tr"), 2 * i)
                            kb.c(sc.activation(xnt[:, :], xt[sl][:, :], AF.Square,
                                               accum_out=stats[:, ba + 1:ba + 2]), "p1_sq")
                            if i == 0:
                                sc.wait_ge(kb.sem("c_ld"), 7 * 16)
                            sc.wait_ge(kb.sem("p1_s2"), i + 1)
                            kb.c(sc.activation(stats[:, ba + 5:ba + 6], stats[:, ba + 4:ba + 5],
                                               AF.Sqrt, bias=epsb[:, 0:1]), "p1_sqrt")
                            sc.wait_ge(kb.sem("p1_s3"), i + 1)
                            kb.c(sc.activation(xnt[:, :], xt[sl][:, :], AF.Identity,
                                               bias=stats[:, ba + 6:ba + 7],
                                               scale=stats[:, ba + 7:ba + 8]), "p1_xn")
                            if sl == 1 and i + 2 < NT:
                                sc.wait_ge(kb.sem("p1_xn"), kb.val["p1_xn"])
                                kb.d(sc.dma_start(out=xt[1][:, :],
                                                  in_=x_full[(i + 2) * 128:(i + 3) * 128, :]),
                                     "p1_x1")
                            if i >= 1:
                                _p1_evac(nc, kb, sc, i - 1, psTr, xn_fm)
                        _p1_evac(nc, kb, sc, NT - 1, psTr, xn_fm)

            _emit_p2(nc, kb, dict(
                kt=kt, qt=qt_sb, vt=vt, xn_fm=xn_fm, sel2=sel2,
                wk_r=wk_r, wq_r=wq_r, wv_r=wv_r, wdn_r=wdn_r, sdn_r=sdn_r,
                wdnd_dram=wdnd_dram, vones=vones_in, xno_dram=xno_dram,
                wup_r=wup_r, sup_r=sup_r, wupd_dram=wupd_dram))
            PX.close()

            _emit_p3(nc, kb, dict(
                kt=kt, qt=qt_sb, vt=vt, consts=consts, ident=ident,
                t_dram=t_dram, av_loc=av_loc, rd_dram=rd_dram))

        with ExitStack() as PF2:
            E5 = PF2.enter_context
            x2t = E5(nc.sbuf_tensor("x2t", [128, NHC * 1024], BF16))
            t2sb = E5(nc.sbuf_tensor("t2sb", [RLO, 1024], BF16))
            _emit_p5(nc, kb, dict(
                x2t=x2t, t2sb=t2sb, bupc=bupc_sb,
                xno_dram=xno_dram, wupd_dram=wupd_dram, bupl_r=bupl_r,
                aup_r=aup_r, adn_r=adn_r, av_loc=av_loc, av_g=av_g,
                bar_src=bar_src, bar_dst=bar_dst))
            _emit_p6(nc, kb, dict(
                x2t=x2t, t2sb=t2sb, ones_bf=ones_bf,
                wdnd_dram=wdnd_dram, av_g=av_g, wo_r=wo_r, bdn_row=bdn_row,
                bdnl_r=bdnl_r, x_own=x_own, out_t=out_t))
    return nc


def _p1_evac(nc, kb, sc, i, psTr, xn_fm):
    for hb in range(2):
        sc.wait_ge(kb.sem("p1_tr"), 2 * i + hb + 1)
        dst = bass.AP(xn_fm, hb * 8 * 2048 + i * 128,
                      [[NDC * 2048, 128], [2048, 8], [1, 128]])
        src = bass.AP(psTr, hb * 1024, [[2048, 128], [128, 8], [1, 128]])
        kb.c(sc.activation(dst, src, AF.Identity), "p1_ev")


def _emit_p2(nc, kb, tn):
    kt, qt, vt, xn_fm, sel2 = tn["kt"], tn["qt"], tn["vt"], tn["xn_fm"], tn["sel2"]
    wk_r, wq_r, wv_r = tn["wk_r"], tn["wq_r"], tn["wv_r"]
    wdn_r, sdn_r, wdnd_dram = tn["wdn_r"], tn["sdn_r"], tn["wdnd_dram"]
    vones, xno_dram_t = tn["vones"], tn["xno_dram"]
    wup_r, sup_r, wupd_dram = tn["wup_r"], tn["sup_r"], tn["wupd_dram"]

    def _q_src(item):
        if item < NHC:
            hcx = item
            return (wdn_r[hcx, :, :], 2048, sdn_r[hcx, :, :],
                    bass.AP(wdnd_dram, hcx * 256,
                            [[8 * NHC * 256, 128], [NHC * 256, 8], [1, 256]]))
        p = item - NHC
        hg, half = p // 2, p % 2
        return (bass.AP(wup_r, hg * (128 * 4096) + half * 2048, [[4096, 128], [1, 2048]]),
                2048,
                bass.AP(sup_r, hg * 8192 + half * 2048, [[4096, 2], [1, 2048]]),
                bass.AP(wupd_dram, hg * (128 * 4096) + half * 2048,
                        [[4096, 128], [1, 2048]]))
    with ExitStack() as P:
        E = P.enter_context
        wbuf = [E(nc.sbuf_tensor(f"wbuf{i}", [128, 2048], BF16)) for i in range(2)]
        wvbuf = E(nc.sbuf_tensor("wvbuf", [128, NDC * 256], BF16))
        stg = [E(nc.sbuf_tensor(f"stg{i}", [128, D], BF16)) for i in range(2)]
        sdsb = [E(nc.sbuf_tensor(f"sdsb{i}", [2, D], BF16)) for i in range(2)]
        wdd = [E(nc.sbuf_tensor(f"wdd{i}", [128, D], BF16)) for i in range(2)]
        psP = E(nc.psum_tensor("psP", [128, 2048], F32))
        psV = [E(nc.psum_tensor(f"psV{i}", [128, 512], F32)) for i in range(2)]
        psSc = [E(nc.psum_tensor(f"psSc{i}", [128, 512], F32)) for i in range(2)]
        with nc.Block() as blk:
            @blk.gpsimd
            def _(g):
                for kc in range(NQT):
                    kb.d(g.dma_start(out=vt[:, kc * 1040:(kc + 1) * 1040],
                                     in_=vones[:, :]), "p2_on")
                for w in range(16):
                    sl = w % 2
                    if w >= 2:
                        g.wait_ge(kb.sem("p2_proj"), w - 1)
                    wsrc = wk_r if w < 8 else wq_r
                    tt = w % 8
                    ap = bass.AP(wsrc, tt * (128 * NDC * 128), [[NDC * 128, 128], [1, NDC * 128]])
                    kb.d(g.dma_start(out=wbuf[sl][:, :], in_=ap, max_dma_last_dim=8192),
                         "p2_w%d" % sl)
                    for item in range(8 * w, 8 * w + 8):
                        hsl = item % 2
                        if item >= 2:
                            g.wait_ge(kb.sem("p2_deq"), 4 * (item - 1))
                        sap, mdld, scap, _ = _q_src(item)
                        kb.d(g.dma_start(out=stg[hsl][:, :], in_=sap,
                                         max_dma_last_dim=mdld), "p2_stg%d" % hsl)
                        kb.d(g.dma_start(out=sdsb[hsl][0:2, :], in_=scap,
                                         max_dma_last_dim=8192), "p2_stg%d" % hsl)
                for fq in range(4):
                    if fq >= 1:
                        g.wait_ge(kb.sem("p2_vkc"), 16 * fq)
                    ap = bass.AP(wv_r, fq * (128 * NDC * 256), [[NDC * 256, 128], [1, NDC * 256]])
                    kb.d(g.dma_start(out=wvbuf[:, :], in_=ap, max_dma_last_dim=8192), "p2_wv")

            @blk.sync
            def _(s):
                s.wait_ge(kb.sem("p1_ev"), 2 * NQT)
                kb.d(s.dma_start(out=xno_dram_t[:, :, :], in_=xn_fm[:, :]), "p1_pk")
                for hcx in range(NHC):
                    sl = hcx % 2
                    s.wait_ge(kb.sem("p2_deq"), 4 * (hcx + 1))
                    ap = bass.AP(wdnd_dram, hcx * 256,
                                 [[8 * NHC * 256, 128], [NHC * 256, 8], [1, 256]])
                    kb.d(s.dma_start(out=ap, in_=wdd[sl][:, :]), "p2_dst%d" % sl)
                if True:
                    s.wait_ge(kb.sem("p2_dst0"), kb.val["p2_dst0"])
                    s.wait_ge(kb.sem("p2_dst1"), kb.val["p2_dst1"])

            @blk.tensor
            def _(t):
                for w in range(16):
                    sl = w % 2
                    t.wait_ge(kb.sem("p2_w%d" % sl), 16 * (w // 2 + 1))
                    if w == 0:
                        t.wait_ge(kb.sem("p1_ev"), 2 * NQT)
                    if w >= 1:
                        t.wait_ge(kb.sem("p2_evp"), w)
                    for dc in range(NDC):
                        for bk in range(4):
                            ins = t.matmul(psP[:, bk * 512:(bk + 1) * 512],
                                           wbuf[sl][:, dc * 128:(dc + 1) * 128],
                                           xn_fm[:, dc * 2048 + bk * 512: dc * 2048 + (bk + 1) * 512],
                                           start=(dc == 0), stop=(dc == NDC - 1))
                    kb.c(ins, "p2_proj")
                    for item in range(8 * w, 8 * w + 8):
                        ssl = item % 2
                        t.wait_ge(kb.sem("p2_stg%d" % ssl), 32 * (item // 2 + 1))
                        for k in range(4):
                            m = 4 * item + k
                            if m >= 2:
                                t.wait_ge(kb.sem("p2_deq"), m - 1)
                            kb.c(t.matmul(psSc[m % 2][:, :], sel2[0:2, :],
                                          sdsb[ssl][0:2, k * 512:(k + 1) * 512],
                                          start=True, stop=True), "p2_scm")
                for fq in range(4):
                    t.wait_ge(kb.sem("p2_wv"), 16 * (fq + 1))
                    for kc in range(NQT):
                        vi = 16 * fq + kc
                        if vi >= 2:
                            t.wait_ge(kb.sem("p2_evv"), 4 * (vi - 1))
                        for dc in range(NDC):
                            ins = t.matmul(psV[vi % 2][:, 0:256],
                                           xn_fm[:, dc * 2048 + kc * 128: dc * 2048 + (kc + 1) * 128],
                                           wvbuf[:, dc * 256:(dc + 1) * 256],
                                           start=(dc == 0), stop=(dc == NDC - 1))
                        kb.c(ins, "p2_vkc")

            @blk.vector
            def _(v):
                for item in range(2 * NHC):
                    sl = item % 2
                    v.wait_ge(kb.sem("p2_stg%d" % sl), 32 * (item // 2 + 1))
                    if item >= 2:
                        prev = item - 2
                        nm = ("p2_dst%d" % (prev % 2)) if prev < NHC else \
                             ("p2_ust%d" % (prev % 2))
                        cnt = (prev // 2 + 1) if prev < NHC else ((prev - NHC) // 2 + 1)
                        v.wait_ge(kb.sem(nm), 16 * cnt)
                    for k in range(4):
                        v.wait_ge(kb.sem("p2_scm"), 4 * item + k + 1)
                        kb.c(v.scalar_tensor_tensor(wdd[sl][:, k * 512:(k + 1) * 512],
                                                    stg[sl][:, k * 512:(k + 1) * 512],
                                                    7.5, psSc[(4 * item + k) % 2][:, :],
                                                    AluOpType.subtract, AluOpType.mult), "p2_deq")

            @blk.scalar
            def _(sc):
                for w in range(16):
                    sc.wait_ge(kb.sem("p2_proj"), w + 1)
                    dst = kt if w < 8 else qt
                    kb.c(sc.activation(dst[:, (w % 8) * 2048:(w % 8 + 1) * 2048],
                                       psP[:, :], AF.Identity), "p2_evp")
                    if w >= 8:
                        for item in range(8 * w, 8 * w + 8):
                            sl = item % 2
                            sc.wait_ge(kb.sem("p2_deq"), 4 * (item + 1))
                            _, _, _, dap = _q_src(item)
                            kb.d(sc.dma_start(out=dap, in_=wdd[sl][:, :]),
                                 "p2_ust%d" % sl)
                for fq in range(4):
                    for kc in range(NQT):
                        vi = 16 * fq + kc
                        if vi == 0:
                            sc.wait_ge(kb.sem("p2_on"), 16 * NQT)
                        sc.wait_ge(kb.sem("p2_vkc"), vi + 1)
                        for j in range(4):
                            hcol = kc * 1040 + (fq * 4 + j) * 65
                            kb.c(sc.activation(vt[:, hcol:hcol + 64],
                                               psV[vi % 2][:, j * 64:(j + 1) * 64],
                                               AF.Identity), "p2_evv")
                sc.wait_ge(kb.sem("p2_ust0"), kb.val["p2_ust0"])
                sc.wait_ge(kb.sem("p2_ust1"), kb.val["p2_ust1"])


def _emit_p3(nc, kb, tn):
    kt_sb, qt, vt = tn["kt"], tn["qt"], tn["vt"]
    consts, ident = tn["consts"], tn["ident"]
    t_dram, av_loc, rd_dram = tn["t_dram"], tn["av_loc"], tn["rd_dram"]

    def regions(half, kt):
        W0, W1 = half * 1024, half * 1024 + 1024
        out = []
        for lo, hi, kind in ((0, kt * 128, "c0"),
                             (kt * 128, min(kt * 128 + 256, 2048), "var"),
                             (kt * 128 + 256, 2048, "c31")):
            l, h2 = max(lo, W0), min(hi, W1)
            if l < h2:
                out.append((l - W0, h2 - W0, kind))
        return out

    def bias_tiles(half, kt):
        tiles = []
        for b, sec in ((kt, 0), (kt + 1, 1)):
            if b <= 15 and 8 * half <= b < 8 * half + 8:
                tiles.append((b - 8 * half, sec))
        if len(tiles) == 2 and (tiles[0][0] * 128) % 512 != 384:
            return [(tiles[0][0] * 128, 256, 0)]
        return [(t[0] * 128, 128, t[1]) for t in tiles]

    with ExitStack() as P:
        E = P.enter_context
        bias_sb = E(nc.sbuf_tensor("bias_sb", [128, NHL * 256], BF16))
        PT = [E(nc.sbuf_tensor(f"pt{i}", [128, 1024], BF16)) for i in range(2)]
        attn_sb = [E(nc.sbuf_tensor(f"attn_sb{i}", [64, 2048], BF16)) for i in range(2)]
        rd_sb = E(nc.sbuf_tensor("rd_sb", [128, 2048], F32))
        rdraw = [E(nc.sbuf_tensor(f"rdraw{i}", [64, 1024], F32)) for i in range(2)]
        rdb = [E(nc.sbuf_tensor(f"rdb{i}", [64, 1024], F32)) for i in range(2)]
        psS2 = [E(nc.psum_tensor(f"psS2_{i}", [128, 1024], F32)) for i in range(2)]
        psAV = [E(nc.psum_tensor(f"psAV{i}", [65, 1024], F32)) for i in range(2)]
        with nc.Block() as blk:
            @blk.sync
            def _(s):
                for h in range(NHL):
                    for sec in range(2):
                        ap = bass.AP(t_dram, h * 65536 + sec * 32768, [[256, 128], [1, 128]])
                        kb.d(s.dma_start(out=bias_sb[:, h * 256 + sec * 128:
                                                     h * 256 + sec * 128 + 128],
                                         in_=ap), "p3_bias")
                for u in range(32):
                    h, half = u // 2, u % 2
                    s.wait_ge(kb.sem("p3_dnv"), u + 1)
                    kb.d(s.dma_start(out=rd_dram[u:u + 1, :],
                                     in_=rd_sb[64:65, half * 1024:half * 1024 + 1024]),
                         "p3_dw")
                    if u >= 2:
                        s.wait_ge(kb.sem("p3_rc"), u - 1)
                    s.wait_ge(kb.sem("p3_dw"), 16 * (u + 1))
                    ap = bass.AP(rd_dram, u * 1024, [[0, 64], [1, 1024]])
                    kb.d(s.dma_start(out=rdraw[u % 2][0:64, :], in_=ap), "p3_db")
                    if half == 1:
                        s.wait_ge(kb.sem("p3_nm"), u + 1)
                        ap2 = bass.AP(av_loc, h * 64 * S, [[S, 64], [1, S]])
                        kb.d(s.dma_start(out=ap2, in_=attn_sb[h % 2][0:64, :]), "p3_st")
                s.wait_ge(kb.sem("p3_st"), 16 * 16)

            def scores(t, g):
                u, ktn = g // 16, g % 16
                h, half = u // 2, u % 2
                hp, hs = h // 2, h % 2
                if g == 0:
                    t.wait_ge(kb.sem("p3_bias"), 16 * 32)
                    t.wait_ge(kb.sem("p2_evp"), 16)
                    t.wait_ge(kb.sem("p0_cc"), 1)
                if g >= 2:
                    t.wait_ge(kb.sem("p3_exp"), g - 1)
                qbase = hp * 2048 + half * 1024
                writes = [("sc", 0), ("sc", 1)]
                for (L, wdt, sec) in bias_tiles(half, ktn):
                    writes.append(("bias", L, wdt, sec))
                bank_of = lambda w: w[1] if w[0] == "sc" else w[1] // 512
                last = {}
                for idx, w in enumerate(writes):
                    last[bank_of(w)] = idx
                ins = None
                for idx, w in enumerate(writes):
                    stop = (last[bank_of(w)] == idx)
                    if w[0] == "sc":
                        bk = w[1]
                        ins = t.matmul(psS2[g % 2][:, bk * 512:(bk + 1) * 512],
                                       kt_sb[hs * 64:hs * 64 + 64,
                                             hp * 2048 + ktn * 128: hp * 2048 + (ktn + 1) * 128],
                                       qt[hs * 64:hs * 64 + 64,
                                          qbase + bk * 512: qbase + (bk + 1) * 512],
                                       start=True, stop=stop)
                    else:
                        _, L, wdt, sec = w
                        ins = t.matmul(psS2[g % 2][:, L:L + wdt], ident[:, :],
                                       bias_sb[:, h * 256 + sec * 128:
                                               h * 256 + sec * 128 + wdt],
                                       start=False, stop=stop)
                kb.c(ins, "p3_sc")

            def av(t, g):
                u, ktn = g // 16, g % 16
                h, half = u // 2, u % 2
                t.wait_ge(kb.sem("p3_exp"), g + 1)
                if g == 0:
                    t.wait_ge(kb.sem("p2_evv"), 256)
                    t.wait_ge(kb.sem("p2_on"), 16 * NQT)
                if ktn == 0 and u >= 2:
                    t.wait_ge(kb.sem("p3_nm"), u - 1)
                ins = None
                for bk in range(2):
                    ins = t.matmul(psAV[half][:, bk * 512:(bk + 1) * 512],
                                   vt[:, ktn * 1040 + h * 65: ktn * 1040 + h * 65 + 65],
                                   PT[g % 2][:, bk * 512:(bk + 1) * 512],
                                   start=(ktn == 0), stop=(ktn == 15))
                kb.c(ins, "p3_avk")

            @blk.tensor
            def _(t):
                for it in range(514):
                    if it < 512:
                        scores(t, it)
                    if it >= 2:
                        av(t, it - 2)

            @blk.scalar
            def _(sc):
                for g in range(512):
                    u, ktn = g // 16, g % 16
                    h, half = u // 2, u % 2
                    sc.wait_ge(kb.sem("p3_sc"), g + 1)
                    if g >= 2:
                        sc.wait_ge(kb.sem("p3_avk"), g - 1)
                    ins = None
                    for (lo, hi, kind) in regions(half, ktn):
                        if kind == "var":
                            ins = sc.activation(PT[g % 2][:, lo:hi],
                                                psS2[g % 2][:, lo:hi], AF.Exp)
                        else:
                            col = 2 * h if kind == "c0" else 2 * h + 1
                            ins = sc.activation(PT[g % 2][:, lo:hi],
                                                psS2[g % 2][:, lo:hi], AF.Exp,
                                                bias=consts[:, col:col + 1])
                    kb.c(ins, "p3_exp")

            @blk.vector
            def _(v):
                for u in range(32):
                    h, half = u // 2, u % 2
                    v.wait_ge(kb.sem("p3_avk"), 16 * (u + 1))
                    kb.c(v.tensor_copy(rd_sb[64:65, half * 1024:half * 1024 + 1024],
                                       psAV[half][64:65, :]), "p3_dnv")
                    v.wait_ge(kb.sem("p3_db"), 16 * (u + 1))
                    if u >= 2:
                        v.wait_ge(kb.sem("p3_nm"), u - 1)
                    kb.c(v.reciprocal(rdb[u % 2][0:64, :], rdraw[u % 2][0:64, :]), "p3_rc")
                    v.wait_ge(kb.sem("p3_rc"), kb.val["p3_rc"])
                    if half == 0 and h >= 2:
                        v.wait_ge(kb.sem("p3_st"), 16 * (h - 1))
                    kb.c(v.tensor_tensor(attn_sb[h % 2][0:64, half * 1024:half * 1024 + 1024],
                                         psAV[half][0:64, :], rdb[u % 2][0:64, :],
                                         AluOpType.mult), "p3_nm")


def _emit_p5(nc, kb, tn):
    x2t, t2sb, bupc = tn["x2t"], tn["t2sb"], tn["bupc"]
    xno_dram, wupd_dram = tn["xno_dram"], tn["wupd_dram"]
    bupl_r, aup_r, adn_r = tn["bupl_r"], tn["aup_r"], tn["adn_r"]
    av_loc, av_g = tn["av_loc"], tn["av_g"]
    bar_src, bar_dst = tn["bar_src"], tn["bar_dst"]
    with ExitStack() as P:
        E = P.enter_context
        xno = E(nc.sbuf_tensor("xno", [128, NDC * 1024], BF16))
        wud = [E(nc.sbuf_tensor(f"wud{i}", [128, NDC * 256], BF16)) for i in range(2)]
        buplb = [E(nc.sbuf_tensor(f"buplb{i}", [RLO, 256], BF16)) for i in range(2)]
        aupb = E(nc.sbuf_tensor("aupb", [128, NDC * RLO], BF16))
        adnb = E(nc.sbuf_tensor("adnb", [128, NHC * RLO], BF16))
        t1sb = E(nc.sbuf_tensor("t1sb", [RLO, 1024], BF16))
        psTl = [E(nc.psum_tensor(f"psTl{i}", [RLO, 512], F32)) for i in range(2)]
        psY1 = [E(nc.psum_tensor(f"psY1_{i}", [128, 1024], F32)) for i in range(2)]
        with nc.Block() as blk:
            @blk.gpsimd
            def _(g):
                kb.d(g.dma_start(out=aupb[:, :], in_=aup_r[:, :]), "p5_ld")
                kb.d(g.dma_start(out=adnb[:, :], in_=adn_r[:, :], max_dma_last_dim=4096), "p5_ld")

                def wload(hg):
                    sl = hg % 2
                    if hg >= 2:
                        g.wait_ge(kb.sem("p5_up"), 2 * (hg - 1))
                    g.wait_ge(kb.sem("p2_ust0"), 16 * (hg + 1))
                    g.wait_ge(kb.sem("p2_ust1"), 16 * (hg + 1))
                    kb.d(g.dma_start(out=wud[sl][:, :], in_=wupd_dram[hg, :, :]), "p5_stg")
                    ap = bass.AP(bupl_r, hg * (RLO * 256), [[256, RLO], [1, 256]])
                    kb.d(g.dma_start(out=buplb[sl][0:RLO, :], in_=ap), "p5_stg")
                wload(0)
                wload(1)
                g.wait_ge(kb.sem("p3_st"), 16 * 16)
                kb.c(g.collective_compute(
                    "AllGather", AluOpType.bypass,
                    replica_groups=[[2 * i, 2 * i + 1] for i in range(4)],
                    ins=[av_loc.ap().opt()],
                    outs=[av_g.ap().opt()]), "p5_cc")
                kb.c(g.collective_compute(
                    "AllGather", AluOpType.bypass,
                    replica_groups=[[2 * i, 2 * i + 1] for i in range(4)],
                    ins=[bar_src.ap().opt()],
                    outs=[bar_dst.ap().opt()]), "p5_cc")
                for hg in range(2, 32):
                    wload(hg)

            @blk.sync
            def _(s):
                s.wait_ge(kb.sem("p1_pk"), 16)
                pid = s.partition_id()
                hoff = s.compute_val((pid % 2) * 1024)
                kb.d(s.dma_start(out=xno[:, :],
                                 in_=xno_dram[:, :, bass.ds(hoff, 1024)]), "p5_xno")

            @blk.tensor
            def _(t):
                t.wait_ge(kb.sem("p5_xno"), 16)
                t.wait_ge(kb.sem("p5_ld"), 32)
                for half in range(2):
                    for dc in range(NDC):
                        ins = t.matmul(psTl[half][:, :], aupb[:, dc * RLO:(dc + 1) * RLO],
                                       xno[:, dc * 1024 + half * 512: dc * 1024 + (half + 1) * 512],
                                       start=(dc == 0), stop=(dc == NDC - 1))
                    kb.c(ins, "p5_t1")
                for hg in range(32):
                    sl = hg % 2
                    t.wait_ge(kb.sem("p5_stg"), 32 * (hg + 1))
                    for hc2 in range(2):
                        hcx = 2 * hg + hc2
                        if hcx >= 2:
                            t.wait_ge(kb.sem("p5_rel"), hcx - 1)
                        if hcx == 0:
                            t.wait_ge(kb.sem("p5_t1e"), 2)
                        for half in range(2):
                            for dc in range(NDC):
                                t.matmul(psY1[hcx % 2][:, half * 512:(half + 1) * 512],
                                         wud[sl][:, dc * 256 + hc2 * 128: dc * 256 + hc2 * 128 + 128],
                                         xno[:, dc * 1024 + half * 512: dc * 1024 + (half + 1) * 512],
                                         start=(dc == 0), stop=False)
                            ins = t.matmul(psY1[hcx % 2][:, half * 512:(half + 1) * 512],
                                           buplb[sl][0:RLO, hc2 * 128: hc2 * 128 + 128],
                                           t1sb[0:RLO, half * 512:(half + 1) * 512],
                                           start=False, stop=True)
                        kb.c(ins, "p5_up")
                        if hcx >= 1:
                            _p5_t2mm(nc, kb, t, hcx - 1, psTl, adnb, x2t)
                _p5_t2mm(nc, kb, t, NHC - 1, psTl, adnb, x2t)

            @blk.vector
            def _(v):
                for half in range(2):
                    v.wait_ge(kb.sem("p5_t1"), half + 1)
                    kb.c(v.tensor_copy(t1sb[0:RLO, half * 512:(half + 1) * 512],
                                       psTl[half][:, :]), "p5_t1e")
                v.wait_ge(kb.sem("p5_t2"), NHC)
                for half in range(2):
                    kb.c(v.tensor_copy(t2sb[0:RLO, half * 512:(half + 1) * 512],
                                       psTl[half][:, :]), "p5_t2e")

            @blk.scalar
            def _(sc):
                for hcx in range(NHC):
                    sc.wait_ge(kb.sem("p5_up"), hcx + 1)
                    kb.c(sc.activation(x2t[:, hcx * 1024:(hcx + 1) * 1024],
                                       psY1[hcx % 2][:, :], AF.Relu,
                                       bias=bupc[:, hcx:hcx + 1]), "p5_rel")


def _p5_t2mm(nc, kb, t, hcx, psTl, adnb, x2t):
    t.wait_ge(kb.sem("p5_rel"), hcx + 1)
    if hcx == 0:
        t.wait_ge(kb.sem("p5_t1e"), 2)
    for half in range(2):
        ins = t.matmul(psTl[half][:, :], adnb[:, hcx * RLO:(hcx + 1) * RLO],
                       x2t[:, hcx * 1024 + half * 512: hcx * 1024 + (half + 1) * 512],
                       start=(hcx == 0), stop=(hcx == NHC - 1))
    kb.c(ins, "p5_t2")


def _emit_p6(nc, kb, tn):
    x2t, t2sb, ones_bf = tn["x2t"], tn["t2sb"], tn["ones_bf"]
    wdnd_dram, av_g, wo_r = tn["wdnd_dram"], tn["av_g"], tn["wo_r"]
    bdn_row, bdnl_r, x_own, out_t = tn["bdn_row"], tn["bdnl_r"], tn["x_own"], tn["out_t"]
    with ExitStack() as P:
        E = P.enter_context
        wdnsb = [E(nc.sbuf_tensor(f"wdnsb{i}", [128, 32 * 256], BF16)) for i in range(2)]
        avb = [E(nc.sbuf_tensor(f"avb{i}", [128, NDC * 128], BF16)) for i in range(2)]
        wob = [E(nc.sbuf_tensor(f"wob{i}", [128, NDC * 256], BF16)) for i in range(2)]
        bdnb = E(nc.sbuf_tensor("bdnb", [1, D], BF16))
        bdnlb = E(nc.sbuf_tensor("bdnlb", [RLO, D], BF16))
        xrb = [E(nc.sbuf_tensor(f"xrb{i}", [128, 256], F32)) for i in range(2)]
        outb = [E(nc.sbuf_tensor(f"outb{i}", [128, 256], F32)) for i in range(2)]
        psY = [E(nc.psum_tensor(f"psY{i}", [128, 512], F32)) for i in range(2)]
        with nc.Block() as blk:
            @blk.sync
            def _(s):
                s.wait_ge(kb.sem("p5_cc"), 2)
                s.wait_ge(kb.sem("p5_rel"), 56)
                pid = s.partition_id()
                hoff = s.compute_val((pid % 2) * 1024)
                coffs = [s.compute_val(hoff + tt * 128) for tt in range(8)]
                for dg in range(8):
                    for tt in range(8):
                        gi = dg * 8 + tt
                        if gi >= 2:
                            s.wait_ge(kb.sem("p6_mm"), gi - 1)
                        ap = av_g[:, bass.ds(coffs[tt], 128)].rearrange(
                            "(dc p) j -> p dc j", p=128)
                        kb.d(s.dma_start(out=avb[gi % 2][:, :], in_=ap),
                             "p6_av%d" % (gi % 2))

            @blk.scalar
            def _(sc):
                def wdn_load(idx):
                    dg, hx = idx // 2, idx % 2
                    if dg >= 1:
                        sc.wait_ge(kb.sem("p6_mmh"), 16 * dg + hx - 1)
                    ap = bass.AP(wdnd_dram, dg * NHC * 256 + hx * 32 * 256,
                                 [[8 * NHC * 256, 128], [1, 32 * 256]])
                    kb.d(sc.dma_start(out=wdnsb[idx % 2][:, :], in_=ap), "p6_wdnh")
                wdn_load(0)
                wdn_load(1)
                for idx in range(2, 16):
                    wdn_load(idx)
                sc.wait_ge(kb.sem("p6_wdnh"), 16 * 16)

            @blk.gpsimd
            def _(g):
                kb.d(g.dma_start(out=bdnb[0:1, :], in_=bdn_row[:, :]), "p6_ld")
                kb.d(g.dma_start(out=bdnlb[0:RLO, :], in_=bdnl_r[:, :]), "p6_ld")
                for dg in range(2):
                    sl = dg % 2
                    ap = bass.AP(wo_r, dg * (128 * NDC * 256), [[NDC * 256, 128], [1, NDC * 256]])
                    kb.d(g.dma_start(out=wob[sl][:, :], in_=ap, max_dma_last_dim=8192),
                         "p6_wo%d" % sl)
                for gi in range(2):
                    dg, tt = gi // 8, gi % 8
                    kb.d(g.dma_start(out=xrb[gi % 2][:, :],
                                     in_=x_own[tt * 128:(tt + 1) * 128,
                                               dg * 256:(dg + 1) * 256]), "p6_xr%d" % (gi % 2))
                for gi in range(64):
                    dg, tt = gi // 8, gi % 8
                    g.wait_ge(kb.sem("p6_out"), gi + 1)
                    kb.d(g.dma_start(out=out_t[tt * 128:(tt + 1) * 128,
                                              dg * 256:(dg + 1) * 256],
                                     in_=outb[gi % 2][:, :]), "p6_st%d" % (gi % 2))
                    if gi + 2 < 64:
                        dg2, tt2 = (gi + 2) // 8, (gi + 2) % 8
                        kb.d(g.dma_start(out=xrb[gi % 2][:, :],
                                         in_=x_own[tt2 * 128:(tt2 + 1) * 128,
                                                   dg2 * 256:(dg2 + 1) * 256]),
                             "p6_xr%d" % (gi % 2))
                    if tt == 7 and dg + 2 < 8:
                        sl = dg % 2
                        ap = bass.AP(wo_r, (dg + 2) * (128 * NDC * 256),
                                     [[NDC * 256, 128], [1, NDC * 256]])
                        kb.d(g.dma_start(out=wob[sl][:, :], in_=ap, max_dma_last_dim=8192),
                             "p6_wo%d" % sl)
                g.wait_ge(kb.sem("p6_st0"), 16 * 32)
                g.wait_ge(kb.sem("p6_st1"), 16 * 32)

            @blk.tensor
            def _(t):
                t.wait_ge(kb.sem("p6_ld"), 32)
                t.wait_ge(kb.sem("p5_rel"), NHC)
                t.wait_ge(kb.sem("p5_t2e"), 2)
                for dg in range(8):
                    sl = dg % 2
                    t.wait_ge(kb.sem("p6_wo%d" % sl), 16 * (dg // 2 + 1))
                    for tt in range(8):
                        gi = dg * 8 + tt
                        t.wait_ge(kb.sem("p6_av%d" % (gi % 2)), 16 * (gi // 2 + 1))
                        if gi >= 2:
                            t.wait_ge(kb.sem("p6_out"), gi - 1)
                        if tt == 0:
                            t.wait_ge(kb.sem("p6_wdnh"), 16 * (2 * dg + 2))
                        for dc in range(NDC):
                            t.matmul(psY[gi % 2][:, 0:256], avb[gi % 2][:, dc * 128:(dc + 1) * 128],
                                     wob[sl][:, dc * 256:(dc + 1) * 256],
                                     start=(dc == 0), stop=False)
                        for hx in range(2):
                            for hc2 in range(32):
                                hcx = hx * 32 + hc2
                                ins = t.matmul(psY[gi % 2][:, 0:256],
                                               x2t[:, hcx * 1024 + tt * 128: hcx * 1024 + tt * 128 + 128],
                                               wdnsb[(2 * dg + hx) % 2][:, hc2 * 256:(hc2 + 1) * 256],
                                               start=False, stop=False)
                            kb.c(ins, "p6_mmh")
                        t.matmul(psY[gi % 2][:, 0:256], ones_bf[0:1, :],
                                 bdnb[0:1, dg * 256:(dg + 1) * 256], start=False, stop=False)
                        kb.c(t.matmul(psY[gi % 2][:, 0:256], t2sb[0:RLO, tt * 128:(tt + 1) * 128],
                                      bdnlb[0:RLO, dg * 256:(dg + 1) * 256],
                                      start=False, stop=True), "p6_mm")

            @blk.vector
            def _(v):
                for gi in range(64):
                    v.wait_ge(kb.sem("p6_mm"), gi + 1)
                    v.wait_ge(kb.sem("p6_xr%d" % (gi % 2)), 16 * (gi // 2 + 1))
                    if gi >= 2:
                        v.wait_ge(kb.sem("p6_st%d" % (gi % 2)), 16 * (gi // 2))
                    kb.c(v.tensor_tensor(outb[gi % 2][:, :], psY[gi % 2][:, 0:256],
                                         xrb[gi % 2][:, :], AluOpType.add), "p6_out")


def kernel(**inputs):
    maps = _per_core_inputs(inputs)
    nc = build_program()
    res = run_bass_kernel_spmd(nc, maps, core_ids=list(range(8)))
    out = np.empty((B, S, D), dtype=np.float32)
    for c in range(8):
        b, H = c // 2, c % 2
        out[b, H * 1024:(H + 1) * 1024, :] = res.results[c]["out"]
    return out
